# version 08
# baseline (speedup 1.0000x reference)
import numpy as np
import contextlib
from collections import defaultdict

import concourse.bass as bass
import concourse.mybir as mybir
from concourse.bass_utils import run_bass_kernel_spmd

F32 = mybir.dt.float32
BF16 = mybir.dt.bfloat16
AF = mybir.ActivationFunctionType
OP = mybir.AluOpType
AX = mybir.AxisListType
AP = bass.AP

T, C, HID = 2048, 1024, 4096
NT, NC, NH = 16, 8, 32
NHEAD, HD = 16, 64
NEG = -1.0e9
MAGIC = 12582912.0

ENGS = ("tensor", "vector", "scalar", "gpsimd", "sync")


class Prog:
    def __init__(self):
        self.phases = [defaultdict(list)]
        self.dma_issued = defaultdict(int)
        self.dma_at_phase = []

    def on(self, eng, fn):
        self.phases[-1][eng].append(fn)

    def dma(self, eng, q, dst, src):
        self.dma_issued[q] += 1
        self.phases[-1][eng].append(
            ("dma", q, lambda e, s, d=dst, sc=src: e.dma_start(out=d, in_=sc).then_inc(s, 16)))

    def barrier(self):
        self.dma_at_phase.append(dict(self.dma_issued))
        self.phases.append(defaultdict(list))


def emit(prog, block, bsem, dsems):
    prog.barrier()
    nph = len(prog.dma_at_phase)

    def has_nondma(p, e):
        return any(not isinstance(op, tuple) for op in prog.phases[p][e])

    incs = [sum(1 for e in ENGS if has_nondma(p, e)) for p in range(nph)]
    btarget = [0] * (nph + 1)
    for p in range(nph):
        btarget[p + 1] = btarget[p] + incs[p]

    def make_body(ename):
        def body(eng):
            for p in range(nph):
                ops = prog.phases[p][ename]
                if not ops:
                    continue
                if p > 0:
                    eng.wait_ge(bsem, btarget[p])
                    prev = prog.dma_at_phase[p - 1]
                    for q, sem in dsems.items():
                        if prev.get(q, 0) > 0:
                            eng.wait_ge(sem, 16 * prev[q])
                last = None
                for op in ops:
                    if isinstance(op, tuple):
                        _, q, fn = op
                        fn(eng, dsems[q])
                    else:
                        r = op(eng)
                        if r is not None:
                            last = r
                if p < nph - 1 and has_nondma(p, ename):
                    assert last is not None, f"phase {p} engine {ename}: no instr"
                    last.then_inc(bsem, 1)
        return body

    for ename in ENGS:
        getattr(block, ename)(make_body(ename))


HOSTIN = [
    ("x_in", [T, C], F32), ("wkT", [C, C], F32), ("wvT", [C, C], F32),
    ("wrT", [C, C], F32), ("woT", [C, C], F32),
    ("colv", [128, 40], F32), ("wexp", [C, HID], F32), ("wconP", [HID, C], F32),
    ("becol", [128, 32], F32), ("ubPcol", [128, 32], F32),
    ("bf_init", [4, 128, 1024], F32),
    ("bf_wex", [4, 4, 4, 128, 512], F32),
    ("bf56_d", [2, 2, 128, 32], F32), ("bf56_o", [2, 2, 128, 32], F32),
    ("l1wq", [C, C], F32), ("l1wk", [C, C], F32), ("l1wv", [C, C], F32), ("l1wo", [C, C], F32),
    ("l1bcol", [128, 16], F32),
    ("brows", [1, 3072], F32),
    ("dbrow", [1, C], F32),
    ("l1fbrow", [1, C], F32), ("l1fw", [C, C], F32),
    ("c_idbf", [128, 128], BF16), ("c_idf", [128, 128], F32),
    ("c_mask", [4, 128, 512], BF16), ("c_pi", [2, 128, 128], F32),
    ("c_ind2", [2, 128], F32),
    ("c_ones512", [1, 512], BF16), ("c_onesf", [1, 128], F32),
]


def build():
    nc = bass.Bass("TRN2", target_bir_lowering=False, debug=False)
    D = {}
    for nm, shp, dt in HOSTIN:
        D[nm] = nc.dram_tensor(nm, shp, dt, kind="ExternalInput").ap()
    xout = nc.dram_tensor("x_out", [T, C], F32, kind="ExternalOutput").ap()
    tap0 = nc.dram_tensor("tap0", [T, C], F32, kind="ExternalOutput").ap()
    tap1 = nc.dram_tensor("tap1", [T, C], F32, kind="ExternalOutput").ap()
    tap2 = nc.dram_tensor("tap2", [T, C], F32, kind="ExternalOutput").ap()
    dbg = nc.dram_tensor("dbg", [128, 2048], F32, kind="ExternalOutput").ap()

    xscr = nc.dram_tensor("xscr", [T, C], F32, kind="Internal").ap()
    wexp_bf = nc.dram_tensor("wexp_bf", [C * HID], BF16, kind="Internal").ap()
    wconP_bf = nc.dram_tensor("wconP_bf", [HID * C], BF16, kind="Internal").ap()
    smscr = nc.dram_tensor("smscr", [128 * 1024], F32, kind="Internal").ap()
    hscr = nc.dram_tensor("hscr", [NH * 128 * 512], BF16, kind="Internal").ap()
    denscr = nc.dram_tensor("denscr", [NHEAD * T], F32, kind="Internal").ap()
    mscr = nc.dram_tensor("mscr", [32], F32, kind="Internal").ap()

    es = contextlib.ExitStack()
    prog = Prog()
    with es:
        def sbt(nm, shp, dt=F32):
            return es.enter_context(nc.sbuf_tensor(nm, shp, dt))

        idbf = sbt("idbf", [128, 128], BF16)
        idf = sbt("idf", [128, 128], F32)
        maskd = sbt("maskd", [128, 4 * 512], BF16)
        pim = sbt("pim", [128, 256], F32)
        ind2 = sbt("ind2", [2, 128], F32)
        ones512 = sbt("ones512", [1, 512], BF16)
        onesf = sbt("onesf", [1, 128], F32)
        colv = sbt("colv_s", [128, 40], F32)
        col56 = sbt("col56", [128, 64], F32)
        becol = sbt("becol_s", [128, 32], F32)
        ubPcol = sbt("ubPcol_s", [128, 32], F32)
        l1bcol = sbt("l1bcol_s", [128, 16], F32)
        brows = sbt("brows_s", [1, 3072], BF16)
        l1fbrow = sbt("l1fbrow_s", [1, 1024], BF16)
        stat = sbt("stat", [128, 8], F32)
        stat2 = sbt("stat2", [16, 4], F32)
        carryN = sbt("carryN", [128, 8], F32)
        carryD = sbt("carryD", [128, 8], F32)
        prevc = sbt("prevc", [128, 8], BF16)
        mm8 = sbt("mm8", [128, 16], F32)
        m82 = sbt("m82", [16, 128], F32)
        negM = sbt("negM", [128, 16], F32)
        den2 = sbt("den2", [2, 2048], F32)
        mrow = sbt("mrow", [1, 32], F32)
        q_sc = sbt("q_sc", [128, 32], F32)
        q_isc = sbt("q_isc", [128, 32], F32)

        GA = sbt("GA", [128, 16384], BF16)
        GB = sbt("GB", [128, 16384], BF16)
        GC = sbt("GC", [128, 16384], BF16)
        GD = sbt("GD", [128, 16384], BF16)
        vtmx = sbt("vtmx", [128, 1040], BF16)
        AAF1 = sbt("AAF1", [128, 4096], F32)
        AAF2 = sbt("AAF2", [128, 4096], F32)
        WB1 = sbt("WB1", [128, 1024], BF16)
        WB2 = sbt("WB2", [128, 1024], BF16)
        ptile = sbt("ptile", [128, 2048], BF16)
        xb = [sbt(f"xb{i}", [128, 1024], F32) for i in range(2)]
        xnt = sbt("xnt", [128, 1024], BF16)

        PA = es.enter_context(nc.psum_tensor("PA", [128, 1024], F32))
        PB = es.enter_context(nc.psum_tensor("PB", [128, 1024], F32))
        PC = es.enter_context(nc.psum_tensor("PC", [128, 512], F32))
        PD = es.enter_context(nc.psum_tensor("PD", [128, 512], F32))
        PE = es.enter_context(nc.psum_tensor("PE", [128, 512], F32))
        PF = es.enter_context(nc.psum_tensor("PF", [128, 512], F32))
        PSLOT = ([(PA, i * 128) for i in range(8)] + [(PB, i * 128) for i in range(8)] +
                 [(PC, i * 128) for i in range(4)] + [(PD, i * 128) for i in range(4)] +
                 [(PE, i * 128) for i in range(4)] + [(PF, i * 128) for i in range(4)])

        bsem = es.enter_context(nc.semaphore("bsem"))
        dA = es.enter_context(nc.semaphore("dA"))
        dB = es.enter_context(nc.semaphore("dB"))
        dsems = {"A": dA, "B": dB}
        block = es.enter_context(nc.Block())

        bar = prog.barrier

        def t_(fn):
            prog.on("tensor", fn)

        def v_(fn):
            prog.on("vector", fn)

        def s_(fn):
            prog.on("scalar", fn)

        def g_(fn):
            prog.on("gpsimd", fn)

        dma = prog.dma
        WSa = AAF1
        WSb = AAF2

        dma("sync", "A", idbf[:], D["c_idbf"])
        dma("sync", "A", idf[:], D["c_idf"])
        for d in range(4):
            dma("sync", "A", maskd[:, d * 512:(d + 1) * 512], D["c_mask"][d])
        dma("sync", "A", pim[:, 0:128], D["c_pi"][0])
        dma("sync", "A", pim[:, 128:256], D["c_pi"][1])
        dma("sync", "A", ind2[:], D["c_ind2"])
        dma("sync", "A", ones512[:], D["c_ones512"])
        dma("sync", "A", onesf[:], D["c_onesf"])
        dma("sync", "A", colv[:], D["colv"])
        dma("sync", "A", becol[:], D["becol"])
        dma("sync", "A", ubPcol[:], D["ubPcol"])
        dma("sync", "A", l1bcol[:], D["l1bcol"])
        dma("sync", "A", WSa[0:1, 0:3072], D["brows"])
        dma("sync", "A", WSb[0:1, 0:1024], D["dbrow"])
        dma("sync", "A", WSb[0:1, 1024:2048], D["l1fbrow"])
        bar()
        s_(lambda e: e.activation(out=colv[:, 0:8], in_=colv[:, 0:8], func=AF.Exp, bias=0.0, scale=1.0))
        v_(lambda e: e.tensor_add(out=WSa[0:1, 2048:3072], in0=WSa[0:1, 2048:3072], in1=WSb[0:1, 0:1024]))
        bar()
        s_(lambda e: e.activation(out=colv[:, 0:8], in_=colv[:, 0:8], func=AF.Exp, bias=0.0, scale=-1.0))
        s_(lambda e: e.activation(out=colv[:, 8:16], in_=colv[:, 8:16], func=AF.Exp, bias=0.0, scale=1.0))
        v_(lambda e: e.tensor_copy(out=brows[0:1, :], in_=WSa[0:1, 0:3072]))
        bar()
        v_(lambda e: e.tensor_copy(out=l1fbrow[0:1, :], in_=WSb[0:1, 1024:2048]))
        bar()

        for i in range(16):
            r, cp = divmod(i, 2)
            buf = WSa[:, 0:2048] if i % 2 == 0 else WSa[:, 2048:4096]
            bfb = GA[:, 0:2048] if i % 2 == 0 else GA[:, 2048:4096]
            dma("sync", "A", buf, D["wexp"][r * 128:(r + 1) * 128, cp * 2048:(cp + 1) * 2048])
            bar()
            v_(lambda e, b=buf, o=bfb: e.tensor_copy(out=o, in_=b))
            bar()
            dma("gpsimd", "B", AP(wexp_bf.tensor, r * 128 * 4096 + cp * 2048,
                                  [[4096, 128], [1, 2048]]), bfb)
        for i in range(16):
            buf = WSa[:, 0:2048] if i % 2 == 0 else WSa[:, 2048:4096]
            bfb = GA[:, 0:2048] if i % 2 == 0 else GA[:, 2048:4096]
            dma("sync", "A", buf, AP(D["wconP"].tensor, i * 256 * 1024,
                                     [[1024, 128], [128 * 1024, 2], [1, 1024]]))
            bar()
            v_(lambda e, b=buf, o=bfb: e.tensor_copy(out=o, in_=b))
            bar()
            dma("gpsimd", "B", AP(wconP_bf.tensor, i * 256 * 1024, [[2048, 128], [1, 2048]]), bfb)
        bar()

        def layer_norm_tile(xsrc, ti, out_bf_ap, out_f32_ap=None):
            xa = xb[ti % 2]
            dma("sync", "A", xa[:], xsrc[ti * 128:(ti + 1) * 128, :])
            bar()
            v_(lambda e, a=xa: e.reduce_sum(out=stat[:, 0:1], in_=a[:], axis=AX.X))
            s_(lambda e, a=xa: e.activation(out=WSb[:, 3072:4096], in_=a[:], func=AF.Square,
                                            bias=0.0, scale=1.0, accum_out=stat[:, 1:2]))
            bar()
            v_(lambda e: e.tensor_scalar(out=stat[:, 2:3], in0=stat[:, 0:1], scalar1=1.0 / C,
                                         scalar2=None, op0=OP.mult))
            bar()
            v_(lambda e: e.tensor_mul(out=stat[:, 3:4], in0=stat[:, 2:3], in1=stat[:, 2:3]))
            bar()
            v_(lambda e: e.tensor_scalar(out=stat[:, 4:5], in0=stat[:, 1:2], scalar1=1.0 / C,
                                         scalar2=stat[:, 3:4], op0=OP.mult, op1=OP.subtract))
            bar()
            v_(lambda e: e.tensor_single_scalar(out=stat[:, 4:5], in_=stat[:, 4:5],
                                                scalar=1.0e-5, op=OP.add))
            bar()
            s_(lambda e: e.activation(out=stat[:, 5:6], in_=stat[:, 4:5], func=AF.Sqrt,
                                      bias=0.0, scale=1.0))
            bar()
            v_(lambda e: e.reciprocal(out=stat[:, 6:7], in_=stat[:, 5:6]))
            bar()
            v_(lambda e, a=xa, o=out_bf_ap: e.tensor_scalar(out=o, in0=a[:], scalar1=stat[:, 2:3],
                                                            scalar2=stat[:, 6:7], op0=OP.subtract, op1=OP.mult))
            if out_f32_ap is not None:
                v_(lambda e, a=xa, o=out_f32_ap: e.tensor_scalar(out=o, in0=a[:], scalar1=stat[:, 2:3],
                                                                 scalar2=stat[:, 6:7], op0=OP.subtract, op1=OP.mult))
            bar()

        def tile_to_fm(src_tile, dst_fn):
            for grp in range(2):
                for k in range(4):
                    t_(lambda e, c=grp * 4 + k, kk=k, st=src_tile: e.matmul(
                        out=PA[:, kk * 128:(kk + 1) * 128],
                        lhsT=st[:, c * 128:(c + 1) * 128], rhs=idbf[:], start=True, stop=True))
                bar()
                for k in range(4):
                    s_(lambda e, c=grp * 4 + k, kk=k, df=dst_fn: e.copy(
                        out=df(c), in_=PA[:, kk * 128:(kk + 1) * 128]))
                bar()

        def load_cast(dst_bf, src_f32_ap):
            dma("sync", "A", WSb[:, 2048:3072], src_f32_ap)
            bar()
            v_(lambda e, d=dst_bf: e.tensor_copy(out=d, in_=WSb[:, 2048:3072]))
            bar()

        def xnv(cc):
            return GA[:, cc * 1024:(cc + 1) * 1024]

        def mixv(cc):
            return GA[:, 8192 + cc * 1024: 8192 + (cc + 1) * 1024]

        def ekv(cc):
            return GB[:, cc * 1024:(cc + 1) * 1024]

        def vv(cc):
            return GB[:, 8192 + cc * 1024: 8192 + (cc + 1) * 1024]

        def sgv(cc):
            return GC[:, cc * 1024:(cc + 1) * 1024]

        def rwv(cc):
            return GC[:, 8192 + cc * 1024: 8192 + (cc + 1) * 1024]

        for half in range(2):
            toff = half * 8
            for ti in range(8):
                layer_norm_tile(D["x_in"], toff + ti, xnt[:])
                tile_to_fm(xnt, lambda c, t=ti: xnv(c)[:, t * 128:(t + 1) * 128])
            for pi, (wname, mixcol) in enumerate([("wkT", 16), ("wvT", 24), ("wrT", 32)]):
                for cc in range(NC):
                    v_(lambda e, c=cc: e.tensor_sub(out=WSa[:, 0:1023], in0=xnv(c)[:, 1:1024],
                                                    in1=xnv(c)[:, 0:1023]))
                    bar()
                    v_(lambda e, c=cc, m=mixcol: e.scalar_tensor_tensor(
                        out=mixv(c)[:, 1:1024], in0=WSa[:, 0:1023], scalar=colv[:, m + c:m + c + 1],
                        in1=xnv(c)[:, 0:1023], op0=OP.mult, op1=OP.add))
                    if half == 0:
                        v_(lambda e, c=cc, m=mixcol: e.tensor_scalar(
                            out=mixv(c)[:, 0:1], in0=xnv(c)[:, 0:1], scalar1=colv[:, m + c:m + c + 1],
                            scalar2=None, op0=OP.mult))
                        bar()
                    else:
                        v_(lambda e, c=cc: e.tensor_copy(out=WSa[:, 1100:1101], in_=prevc[:, c:c + 1]))
                        bar()
                        v_(lambda e, c=cc: e.tensor_sub(out=WSa[:, 1101:1102], in0=xnv(c)[:, 0:1],
                                                        in1=WSa[:, 1100:1101]))
                        bar()
                        v_(lambda e, c=cc, m=mixcol: e.scalar_tensor_tensor(
                            out=mixv(c)[:, 0:1], in0=WSa[:, 1101:1102], scalar=colv[:, m + c:m + c + 1],
                            in1=WSa[:, 1100:1101], op0=OP.mult, op1=OP.add))
                        bar()
                for oc in range(NC):
                    wb = WB1 if oc % 2 == 0 else WB2
                    load_cast(wb[:], AP(D[wname].tensor, oc * 128, [[1024, 128], [131072, 8], [1, 128]]))
                    ps = PA if oc % 2 == 0 else PB
                    for kc in range(NC):
                        for nh2 in range(2):
                            t_(lambda e, k=kc, n=nh2, w=wb, p=ps: e.matmul(
                                out=p[:, n * 512:(n + 1) * 512],
                                lhsT=w[:, k * 128:(k + 1) * 128],
                                rhs=mixv(k)[:, n * 512:(n + 1) * 512],
                                start=(k == 0), stop=(k == NC - 1)))
                    bar()
                    if pi == 0:
                        s_(lambda e, o=oc, p=ps: e.activation(out=ekv(o), in_=p[:, 0:1024],
                                                              func=AF.Exp, bias=0.0, scale=1.0))
                    elif pi == 1:
                        s_(lambda e, o=oc, p=ps: e.copy(out=vv(o), in_=p[:, 0:1024]))
                    else:
                        s_(lambda e, o=oc, p=ps: e.activation(out=sgv(o), in_=p[:, 0:1024],
                                                              func=AF.Sigmoid, bias=0.0, scale=1.0))
                    bar()
            if half == 0:
                for cc in range(NC):
                    v_(lambda e, c=cc: e.tensor_copy(out=prevc[:, c:c + 1], in_=xnv(c)[:, 1023:1024]))
                bar()
            for cc in range(NC):
                ewb = AP(colv, cc, [[40, 128], [0, 1024]])
                v_(lambda e, c=cc: e.tensor_mul(out=WB1[:], in0=ekv(c), in1=vv(c)))
                bar()
                initN = 0.0 if half == 0 else carryN[:, cc:cc + 1]
                initD = 0.0 if half == 0 else carryD[:, cc:cc + 1]
                v_(lambda e, w=ewb, iN=initN: e.tensor_tensor_scan(
                    out=WSa[:, 0:1024], data0=w, data1=WB1[:], initial=iN, op0=OP.mult, op1=OP.add))
                v_(lambda e, w=ewb, c=cc, iD=initD: e.tensor_tensor_scan(
                    out=WSb[:, 0:1024], data0=w, data1=ekv(c), initial=iD, op0=OP.mult, op1=OP.add))
                bar()
                v_(lambda e, c=cc: e.tensor_scalar(out=WSa[:, 2048:3072], in0=ekv(c),
                                                   scalar1=colv[:, 8 + c:8 + c + 1], scalar2=None, op0=OP.mult))
                bar()
                v_(lambda e, c=cc: e.tensor_mul(out=WSa[:, 3072:4096], in0=WSa[:, 2048:3072], in1=vv(c)))
                v_(lambda e: e.tensor_add(out=WSb[:, 1025:2048], in0=WSa[:, 2049:3072], in1=WSb[:, 0:1023]))
                bar()
                v_(lambda e: e.tensor_add(out=WSa[:, 1025:2048], in0=WSa[:, 3073:4096], in1=WSa[:, 0:1023]))
                if half == 0:
                    v_(lambda e: e.tensor_copy(out=WSb[:, 1024:1025], in_=WSa[:, 2048:2049]))
                    bar()
                    v_(lambda e: e.tensor_copy(out=WSa[:, 1024:1025], in_=WSa[:, 3072:3073]))
                else:
                    v_(lambda e, c=cc: e.tensor_add(out=WSb[:, 1024:1025], in0=WSa[:, 2048:2049],
                                                    in1=carryD[:, c:c + 1]))
                    bar()
                    v_(lambda e, c=cc: e.tensor_add(out=WSa[:, 1024:1025], in0=WSa[:, 3072:3073],
                                                    in1=carryN[:, c:c + 1]))
                bar()
                v_(lambda e, c=cc: e.tensor_copy(out=carryN[:, c:c + 1], in_=WSa[:, 1023:1024]))
                v_(lambda e, c=cc: e.tensor_copy(out=carryD[:, c:c + 1], in_=WSb[:, 1023:1024]))
                v_(lambda e: e.reciprocal(out=WSb[:, 1024:2048], in_=WSb[:, 1024:2048]))
                bar()
                v_(lambda e: e.tensor_mul(out=WSa[:, 1024:2048], in0=WSa[:, 1024:2048], in1=WSb[:, 1024:2048]))
                bar()
                v_(lambda e, c=cc: e.tensor_mul(out=rwv(c), in0=WSa[:, 1024:2048], in1=sgv(c)))
                bar()
            for ti in range(8):
                ps = PA if ti % 2 == 0 else PB
                xa = xb[ti % 2]
                dma("gpsimd", "B", xa[:], D["x_in"][(toff + ti) * 128:(toff + ti + 1) * 128, :])
                for kc in range(NC):
                    wb = WB1 if kc % 2 == 0 else WB2
                    load_cast(wb[:], D["woT"][kc * 128:(kc + 1) * 128, :])
                    for nh2 in range(2):
                        t_(lambda e, k=kc, n=nh2, w=wb, p=ps, t=ti: e.matmul(
                            out=p[:, n * 512:(n + 1) * 512],
                            lhsT=rwv(k)[:, t * 128:(t + 1) * 128],
                            rhs=w[:, n * 512:(n + 1) * 512],
                            start=(k == 0), stop=(k == NC - 1)))
                    bar()
                v_(lambda e, p=ps, a=xa: e.tensor_add(out=a[:], in0=a[:], in1=p[:, 0:1024]))
                bar()
                dma("gpsimd", "B", xscr[(toff + ti) * 128:(toff + ti + 1) * 128, :], xa[:])
            bar()

        dma("sync", "A", tap0, xscr)
        bar()
        for bi in range(2):
            for part in range(2):
                pidx = bi * 2 + part
                dma("sync", "A", WSa[:, 0:1024], D["bf_init"][pidx])
                bar()
                bufs = [WSa, WSb]
                cur = 0
                for si in range(4):
                    s = 2 ** (si + 1) if part == 0 else 2 ** (3 - si)
                    nb2 = 16 // s
                    wext = bufs[1 - cur]
                    for w4 in range(4):
                        dma("sync", "A", wext[:, 1024 + w4 * 512:1024 + (w4 + 1) * 512],
                            D["bf_wex"][pidx, si, w4])
                    bar()
                    cb, nb = bufs[cur], bufs[1 - cur]
                    for h in range(2):
                        wd = 0 if h == 0 else 3
                        wo = 1 if h == 0 else 2
                        v_(lambda e, hh=h, w=wd, cbuf=cb, nbuf=nb, wx=wext, ss=s, nn=nb2: e.tensor_mul(
                            out=AP(nbuf, hh * ss * 32, [[4096, 128], [64 * ss, nn], [1, 32 * ss]]),
                            in0=AP(cbuf, hh * ss * 32, [[4096, 128], [64 * ss, nn], [1, 32 * ss]]),
                            in1=AP(wx, 1024 + w * 512, [[4096, 128], [32 * ss, nn], [1, 32 * ss]])))
                        bar()
                        v_(lambda e, hh=h, w=wo, cbuf=cb, nbuf=nb, wx=wext, ss=s, nn=nb2: e.tensor_mul(
                            out=AP(nbuf, 3072, [[4096, 128], [64 * ss, nn], [1, 32 * ss]]),
                            in0=AP(cbuf, (1 - hh) * ss * 32, [[4096, 128], [64 * ss, nn], [1, 32 * ss]]),
                            in1=AP(wx, 1024 + w * 512, [[4096, 128], [32 * ss, nn], [1, 32 * ss]])))
                        bar()
                        v_(lambda e, hh=h, nbuf=nb, ss=s, nn=nb2: e.tensor_add(
                            out=AP(nbuf, hh * ss * 32, [[4096, 128], [64 * ss, nn], [1, 32 * ss]]),
                            in0=AP(nbuf, hh * ss * 32, [[4096, 128], [64 * ss, nn], [1, 32 * ss]]),
                            in1=AP(nbuf, 3072, [[4096, 128], [64 * ss, nn], [1, 32 * ss]])))
                        bar()
                    cur = 1 - cur
                dma("gpsimd", "B", AP(smscr.tensor, 0, [[1024, 128], [1, 1024]]), bufs[cur][:, 0:1024])
                bar()
                g_(lambda e: e.memset(AAF2[:, 0:4096], 0.0))
                bar()
                for q in range(4):
                    dma("sync", "A",
                        AP(AAF2, q * 32 * 4096 + q * 32, [[4096, 32], [128, 32], [1, 32]]),
                        AP(smscr.tensor, q * 1024, [[32, 32], [4096, 32], [1, 32]]))
                bar()
                if part == 0:
                    for st in range(2):
                        dma("sync", "A", col56[:, 0:32], D["bf56_d"][bi, st])
                        dma("sync", "A", col56[:, 32:64], D["bf56_o"][bi, st])
                        bar()
                        for ch in range(32):
                            pp, off = PSLOT[ch]
                            t_(lambda e, c=ch, p=pp, o=off, s2=st: e.matmul(
                                out=p[:, o:o + 128], lhsT=pim[:, s2 * 128:(s2 + 1) * 128],
                                rhs=AAF2[:, c * 128:(c + 1) * 128], start=True, stop=True))
                        bar()
                        v_(lambda e: e.tensor_mul(
                            out=AAF1[:, 0:4096].rearrange("p (c k) -> p c k", k=128),
                            in0=AAF2[:, 0:4096].rearrange("p (c k) -> p c k", k=128),
                            in1=AP(col56, 0, [[64, 128], [1, 32], [0, 128]])))
                        bar()
                        pieces = [(PA, 0, 8), (PB, 8, 8), (PC, 16, 4), (PD, 20, 4), (PE, 24, 4), (PF, 28, 4)]
                        for pt, c0, nch in pieces:
                            v_(lambda e, p=pt, cc0=c0, n=nch: e.tensor_mul(
                                out=AAF2[:, cc0 * 128:(cc0 + n) * 128].rearrange("p (c k) -> p c k", k=128),
                                in0=p[:, 0:n * 128].rearrange("p (c k) -> p c k", k=128),
                                in1=AP(col56, 32 + cc0, [[64, 128], [1, n], [0, 128]])))
                        bar()
                        v_(lambda e: e.tensor_add(out=AAF2[:, 0:4096], in0=AAF2[:, 0:4096],
                                                  in1=AAF1[:, 0:4096]))
                        bar()
                    for grp in range(8):
                        for k in range(4):
                            t_(lambda e, c=grp * 4 + k, kk=k: e.matmul(
                                out=PA[:, kk * 128:(kk + 1) * 128],
                                lhsT=AAF2[:, c * 128:(c + 1) * 128], rhs=idf[:], start=True, stop=True))
                        bar()
                        for k in range(4):
                            s_(lambda e, c=grp * 4 + k, kk=k, b=bi: e.copy(
                                out=GD[:, b * 8192 + c * 128:b * 8192 + (c + 1) * 128],
                                in_=PA[:, kk * 128:(kk + 1) * 128]))
                        bar()
                else:
                    v_(lambda e, b=bi: e.tensor_copy(out=GD[:, b * 8192 + 4096:b * 8192 + 8192],
                                                     in_=AAF2[:, 0:4096]))
                    bar()

        def hb1(k):
            return GB[:, k * 512:(k + 1) * 512]

        def hb2(k):
            return GC[:, k * 512:(k + 1) * 512]

        for qt in range(4):
            tt0 = qt * 4
            for ti in range(4):
                layer_norm_tile(xscr, tt0 + ti, xnt[:])
                tile_to_fm(xnt, lambda c, t=ti: GA[:, c * 512 + t * 128: c * 512 + (t + 1) * 128])
            for fc in range(NH):
                wb = WB1 if fc % 2 == 0 else WB2
                dma("sync", "A", wb[:], AP(wexp_bf.tensor, fc * 128, [[4096, 128], [524288, 8], [1, 128]]))
                bar()
                ps = PC if fc % 2 == 0 else PD
                for kc in range(NC):
                    t_(lambda e, k=kc, w=wb, p=ps: e.matmul(
                        out=p[:, 0:512], lhsT=w[:, k * 128:(k + 1) * 128],
                        rhs=GA[:, k * 512:(k + 1) * 512], start=(k == 0), stop=(k == NC - 1)))
                bar()
                s_(lambda e, f=fc, p=ps: e.activation(out=hb1(f), in_=p[:, 0:512], func=AF.Identity,
                                                      bias=becol[:, f:f + 1], scale=1.0))
                bar()
            for fc in range(NH):
                ps = PC if fc % 2 == 0 else PD
                t_(lambda e, f=fc, p=ps: e.matmul(out=p[:, 0:512],
                                                  lhsT=GD[:, f * 128:(f + 1) * 128],
                                                  rhs=hb1(f), start=True, stop=True))
                bar()
                s_(lambda e, f=fc, p=ps: e.copy(out=hb2(f), in_=p[:, 0:512]))
                bar()
            for ck in range(NH):
                dma("gpsimd", "B", AP(hscr.tensor, ck * 65536, [[512, 128], [1, 512]]), hb2(ck))
            bar()
            for kp in range(NH):
                dma("sync", "A", hb1(kp), AP(hscr.tensor, kp * 2048, [[512, 4], [65536, 32], [1, 512]]))
            bar()
            for kp in range(NH):
                ps = PC if kp % 2 == 0 else PD
                t_(lambda e, f=kp, p=ps: e.matmul(out=p[:, 0:512],
                                                  lhsT=GD[:, 4096 + f * 128:4096 + (f + 1) * 128],
                                                  rhs=hb1(f), start=True, stop=True))
                bar()
                s_(lambda e, f=kp, p=ps: e.activation(out=hb2(f), in_=p[:, 0:512], func=AF.Gelu,
                                                      bias=ubPcol[:, f:f + 1], scale=1.0))
                bar()
            for kp in range(NH):
                dma("gpsimd", "B", AP(hscr.tensor, kp * 65536, [[512, 128], [1, 512]]), hb2(kp))
            bar()
            for ck in range(NH):
                dma("sync", "A", hb1(ck), AP(hscr.tensor, ck * 512, [[65536, 32], [16384, 4], [1, 512]]))
            bar()
            for fc in range(NH):
                ps = PC if fc % 2 == 0 else PD
                t_(lambda e, f=fc, p=ps: e.matmul(out=p[:, 0:512],
                                                  lhsT=GD[:, 8192 + f * 128:8192 + (f + 1) * 128],
                                                  rhs=hb1(f), start=True, stop=True))
                bar()
                s_(lambda e, f=fc, p=ps: e.copy(out=hb2(f), in_=p[:, 0:512]))
                bar()
            for ck in range(NH):
                dma("gpsimd", "B", AP(hscr.tensor, ck * 65536, [[512, 128], [1, 512]]), hb2(ck))
            bar()
            for kp in range(NH):
                dma("sync", "A", hb1(kp), AP(hscr.tensor, kp * 2048, [[512, 4], [65536, 32], [1, 512]]))
            bar()
            for kp in range(NH):
                ps = PC if kp % 2 == 0 else PD
                t_(lambda e, f=kp, p=ps: e.matmul(out=p[:, 0:512],
                                                  lhsT=GD[:, 12288 + f * 128:12288 + (f + 1) * 128],
                                                  rhs=hb1(f), start=True, stop=True))
                bar()
                s_(lambda e, f=kp, p=ps: e.copy(out=hb2(f), in_=p[:, 0:512]))
                bar()
            def wcon_psum(ti2, n2):
                if ti2 == 0:
                    return PA, n2 * 512
                if ti2 == 1:
                    return PB, n2 * 512
                if ti2 == 2:
                    return (PC if n2 == 0 else PD), 0
                return (PE if n2 == 0 else PF), 0
            for kp in range(NH):
                wb = WB1 if kp % 2 == 0 else WB2
                dma("sync", "A", wb[:], AP(wconP_bf.tensor, kp * 128 * 1024, [[1024, 128], [1, 1024]]))
                bar()
                for ti in range(4):
                    for n2 in range(2):
                        pp, oo = wcon_psum(ti, n2)
                        t_(lambda e, f=kp, w=wb, p=pp, o=oo, n=n2, t=ti: e.matmul(
                            out=p[:, o:o + 512], lhsT=hb2(f)[:, t * 128:(t + 1) * 128],
                            rhs=w[:, n * 512:(n + 1) * 512], start=(f == 0), stop=False))
                bar()
            for ti in range(4):
                for n2 in range(2):
                    pp, oo = wcon_psum(ti, n2)
                    t_(lambda e, p=pp, o=oo, n=n2: e.matmul(
                        out=p[:, o:o + 512], lhsT=ones512[0:1, 0:128],
                        rhs=brows[0:1, 2048 + n * 512:2048 + (n + 1) * 512], start=False, stop=True))
            bar()
            for ti in range(4):
                xa = xb[ti % 2]
                dma("sync", "A", xa[:], xscr[(tt0 + ti) * 128:(tt0 + ti + 1) * 128, :])
                bar()
                for n2 in range(2):
                    pp, oo = wcon_psum(ti, n2)
                    v_(lambda e, a=xa, p=pp, o=oo, n=n2: e.tensor_add(
                        out=a[:, n * 512:(n + 1) * 512], in0=a[:, n * 512:(n + 1) * 512],
                        in1=p[:, o:o + 512]))
                    bar()
                dma("gpsimd", "B", xscr[(tt0 + ti) * 128:(tt0 + ti + 1) * 128, :], xa[:])
            bar()

        dma("sync", "A", tap1, xscr)
        bar()
        def qtv(cc):
            return GB[:, cc * 2048:(cc + 1) * 2048]

        def ktv(cc):
            return GC[:, cc * 2048:(cc + 1) * 2048]

        def otv(cc):
            return GA[:, cc * 2048:(cc + 1) * 2048]

        def vtm_ap(ti, off, dims):
            if ti < 15:
                return AP(GD, ti * 1040 + off, dims)
            return AP(vtmx, off, dims)

        for ti in range(NT):
            layer_norm_tile(xscr, ti, xnt[:])
            tile_to_fm(xnt, lambda c, t=ti: GA[:, c * 2048 + t * 128: c * 2048 + (t + 1) * 128])
        for pi2, (wname, bc) in enumerate([("l1wq", 0), ("l1wk", 8)]):
            for oc in range(NC):
                wb = WB1 if oc % 2 == 0 else WB2
                load_cast(wb[:], AP(D[wname].tensor, oc * 128, [[1024, 128], [131072, 8], [1, 128]]))
                for tb in range(2):
                    ps = PA if tb == 0 else PB
                    for kc in range(NC):
                        for nh2 in range(2):
                            t_(lambda e, k=kc, n=nh2, w=wb, p=ps, tbb=tb: e.matmul(
                                out=p[:, n * 512:(n + 1) * 512],
                                lhsT=w[:, k * 128:(k + 1) * 128],
                                rhs=GA[:, k * 2048 + tbb * 1024 + n * 512: k * 2048 + tbb * 1024 + (n + 1) * 512],
                                start=(k == 0), stop=(k == NC - 1)))
                bar()
                dstf = qtv if pi2 == 0 else ktv
                s_(lambda e, o=oc, b=bc, df=dstf: e.activation(
                    out=df(o)[:, 0:1024], in_=PA[:, 0:1024], func=AF.Identity,
                    bias=l1bcol[:, b + o:b + o + 1], scale=1.0))
                s_(lambda e, o=oc, b=bc, df=dstf: e.activation(
                    out=df(o)[:, 1024:2048], in_=PB[:, 0:1024], func=AF.Identity,
                    bias=l1bcol[:, b + o:b + o + 1], scale=1.0))
                bar()
        for ti in range(NT):
            ps = PA if ti % 2 == 0 else PB
            for kc in range(NC):
                wb = WB1 if kc % 2 == 0 else WB2
                load_cast(wb[:], D["l1wv"][kc * 128:(kc + 1) * 128, :])
                for nh2 in range(2):
                    t_(lambda e, k=kc, n=nh2, w=wb, p=ps, t=ti: e.matmul(
                        out=p[:, n * 512:(n + 1) * 512],
                        lhsT=GA[:, k * 2048 + t * 128: k * 2048 + (t + 1) * 128],
                        rhs=w[:, n * 512:(n + 1) * 512], start=(k == 0), stop=False))
                bar()
            for nh2 in range(2):
                t_(lambda e, n=nh2, p=ps: e.matmul(
                    out=p[:, n * 512:(n + 1) * 512], lhsT=ones512[0:1, 0:128],
                    rhs=brows[0:1, n * 512:(n + 1) * 512], start=False, stop=True))
            bar()
            s_(lambda e, t=ti, p=ps: e.copy(
                out=vtm_ap(t, 0, [[16384 if t < 15 else 1040, 128], [65, 16], [1, 64]]),
                in_=p[:, 0:1024]))
            bar()
            g_(lambda e, t=ti: e.memset(
                vtm_ap(t, 64, [[16384 if t < 15 else 1040, 128], [65, 16], [1, 1]]), 1.0))
            bar()
        for cc in range(NC):
            v_(lambda e, c=cc: e.reduce_max(out=mm8[:, c:c + 1], in_=qtv(c), axis=AX.X,
                                            apply_absolute_value=True))
            v_(lambda e, c=cc: e.reduce_max(out=mm8[:, 8 + c:8 + c + 1], in_=ktv(c), axis=AX.X,
                                            apply_absolute_value=True))
        bar()
        t_(lambda e: e.matmul(out=PA[0:16, 0:128], lhsT=mm8[:], rhs=idf[:], start=True, stop=True))
        bar()
        s_(lambda e: e.copy(out=m82[:, :], in_=PA[0:16, 0:128]))
        bar()
        v_(lambda e: e.reduce_max(out=stat2[:, 0:2], in_=m82[:].rearrange("p (h k) -> p h k", k=64),
                                  axis=AX.X))
        bar()
        dma("gpsimd", "B", AP(mscr.tensor, 0, [[2, 16], [1, 2]]), stat2[:, 0:2])
        bar()
        dma("sync", "A", mrow[0:1, 0:32], AP(mscr.tensor, 0, [[32, 1], [1, 32]]))
        bar()
        v_(lambda e: e.tensor_mul(out=mrow[0:1, 0:16], in0=mrow[0:1, 0:16], in1=mrow[0:1, 16:32]))
        bar()
        v_(lambda e: e.tensor_scalar(out=mrow[0:1, 0:16], in0=mrow[0:1, 0:16], scalar1=-8.0,
                                     scalar2=-60.0, op0=OP.mult, op1=OP.max))
        bar()
        t_(lambda e: e.matmul(out=PA[:, 0:16], lhsT=onesf[:], rhs=mrow[0:1, 0:16], start=True, stop=True))
        bar()
        s_(lambda e: e.copy(out=negM[:], in_=PA[:, 0:16]))
        bar()
        dma("sync", "A", dbg[0:128, 2000:2016], negM[:])
        dma("sync", "A", dbg[0:1, 1960:1992], mrow[0:1, 0:32])
        dma("sync", "A", dbg[0:16, 1900:1904], stat2[:])
        bar()
        for h in range(NHEAD):
            hc, hp = h // 2, (h % 2) * 64
            for qb in range(4):
                njs = 4 * qb + 4
                for jg in range(0, njs, 2):
                    for sub in range(2):
                        j = jg + sub
                        if j >= njs:
                            continue
                        ps = PC if sub == 0 else PD
                        t_(lambda e, jj=j, p=ps, c=hc, hh=hp, q=qb: e.matmul(
                            out=p[:, 0:512],
                            lhsT=ktv(c)[hh:hh + 64, jj * 128:(jj + 1) * 128],
                            rhs=qtv(c)[hh:hh + 64, q * 512:(q + 1) * 512],
                            start=True, stop=(jj < 4 * q)))
                        if j >= 4 * qb:
                            t_(lambda e, p=ps, dd=j - 4 * qb: e.matmul(
                                out=p[:, 0:512], lhsT=idbf[:],
                                rhs=maskd[:, dd * 512:(dd + 1) * 512], start=False, stop=True))
                    bar()
                    for sub in range(2):
                        j = jg + sub
                        if j >= njs:
                            continue
                        ps = PC if sub == 0 else PD
                        s_(lambda e, jj=j, p=ps, hh=h: e.activation(
                            out=ptile[:, (jj % 4) * 512:((jj % 4) + 1) * 512], in_=p[:, 0:512],
                            func=AF.Exp, bias=negM[:, hh:hh + 1], scale=0.125))
                    bar()
                    for sub in range(2):
                        j = jg + sub
                        if j >= njs:
                            continue
                        t_(lambda e, jj=j, hh=h, nn=njs: e.matmul(
                            out=PE[0:65, 0:512],
                            lhsT=vtm_ap(jj, hh * 65, [[16384 if jj < 15 else 1040, 128], [1, 65]]),
                            rhs=ptile[:, (jj % 4) * 512:((jj % 4) + 1) * 512],
                            start=(jj == 0), stop=(jj == nn - 1)))
                    bar()
                s_(lambda e, c=hc, hh=hp, q=qb: e.copy(
                    out=otv(c)[hh:hh + 64, q * 512:(q + 1) * 512], in_=PE[0:64, 0:512]))
                v_(lambda e: e.tensor_copy(out=AAF1[64:65, 0:512], in_=PE[64:65, 0:512]))
                bar()
                dma("sync", "A", AP(denscr.tensor, h * 2048 + qb * 512, [[512, 1], [1, 512]]),
                    AAF1[64:65, 0:512])
                bar()
        for cc in range(NC):
            dma("sync", "A", den2[:], AP(denscr.tensor, 2 * cc * 2048, [[2048, 2], [1, 2048]]))
            bar()
            if cc == 0:
                dma("sync", "A", dbg[0:2, 0:1800], den2[0:2, 0:1800])
                bar()
            v_(lambda e: e.reciprocal(out=den2[:], in_=den2[:]))
            bar()
            for ti in range(NT):
                t_(lambda e, t=ti: e.matmul(out=PC[:, 0:128], lhsT=ind2[:],
                                            rhs=den2[:, t * 128:(t + 1) * 128], start=True, stop=True))
                bar()
                v_(lambda e, c=cc, t=ti: e.tensor_mul(
                    out=otv(c)[:, t * 128:(t + 1) * 128],
                    in0=otv(c)[:, t * 128:(t + 1) * 128], in1=PC[:, 0:128]))
                bar()
        for ti in range(NT):
            ps = PA if ti % 2 == 0 else PB
            xa = xb[ti % 2]
            dma("gpsimd", "B", xa[:], xscr[ti * 128:(ti + 1) * 128, :])
            for kc in range(NC):
                wb = WB1 if kc % 2 == 0 else WB2
                load_cast(wb[:], D["l1wo"][kc * 128:(kc + 1) * 128, :])
                for nh2 in range(2):
                    t_(lambda e, k=kc, n=nh2, w=wb, p=ps, t=ti: e.matmul(
                        out=p[:, n * 512:(n + 1) * 512],
                        lhsT=otv(k)[:, t * 128:(t + 1) * 128],
                        rhs=w[:, n * 512:(n + 1) * 512], start=(k == 0), stop=False))
                bar()
            for nh2 in range(2):
                t_(lambda e, n=nh2, p=ps: e.matmul(
                    out=p[:, n * 512:(n + 1) * 512], lhsT=ones512[0:1, 0:128],
                    rhs=brows[0:1, 1024 + n * 512:1024 + (n + 1) * 512], start=False, stop=True))
            bar()
            v_(lambda e, p=ps, a=xa: e.tensor_add(out=a[:], in0=a[:], in1=p[:, 0:1024]))
            bar()
            dma("gpsimd", "B", xscr[ti * 128:(ti + 1) * 128, :], xa[:])
        bar()

        dma("sync", "A", tap2, xscr)
        bar()
        def quantize(dst_bf):
            v_(lambda e: e.reduce_max(out=q_sc[:], in_=WSb[:, 0:1024].rearrange("p (a b) -> p a b", b=32),
                                      axis=AX.X, apply_absolute_value=True))
            bar()
            v_(lambda e: e.tensor_scalar(out=q_sc[:], in0=q_sc[:], scalar1=1.0 / 7.5,
                                         scalar2=1.0e-30, op0=OP.mult, op1=OP.max))
            bar()
            v_(lambda e: e.reciprocal(out=q_isc[:], in_=q_sc[:]))
            bar()
            v_(lambda e: e.tensor_mul(out=WSa[:, 0:1024].rearrange("p (a b) -> p a b", b=32),
                                      in0=WSb[:, 0:1024].rearrange("p (a b) -> p a b", b=32),
                                      in1=AP(q_isc, 0, [[32, 128], [1, 32], [0, 32]])))
            bar()
            s_(lambda e: e.activation(out=WSa[:, 1024:2048], in_=WSa[:, 0:1024], func=AF.Abs,
                                      bias=0.0, scale=1.0))
            bar()
            v_(lambda e: e.tensor_single_scalar(out=WSa[:, 2048:3072], in_=WSa[:, 1024:2048],
                                                scalar=2.0, op=OP.is_ge))
            v_(lambda e: e.tensor_single_scalar(out=WSa[:, 3072:4096], in_=WSa[:, 1024:2048],
                                                scalar=4.0, op=OP.is_ge))
            bar()
            v_(lambda e: e.tensor_scalar(out=WSb[:, 1024:2048], in0=WSa[:, 3072:4096], scalar1=-2.0,
                                         scalar2=8.0, op0=OP.mult, op1=OP.add))
            bar()
            v_(lambda e: e.scalar_tensor_tensor(out=WSb[:, 1024:2048], in0=WSa[:, 2048:3072],
                                                scalar=-4.0, in1=WSb[:, 1024:2048], op0=OP.mult, op1=OP.add))
            bar()
            v_(lambda e: e.tensor_mul(out=WSa[:, 0:1024], in0=WSa[:, 0:1024], in1=WSb[:, 1024:2048]))
            bar()
            v_(lambda e: e.tensor_single_scalar(out=WSa[:, 1024:2048], in_=WSa[:, 0:1024],
                                                scalar=MAGIC, op=OP.add))
            bar()
            v_(lambda e: e.tensor_single_scalar(out=WSa[:, 1024:2048], in_=WSa[:, 1024:2048],
                                                scalar=MAGIC, op=OP.subtract))
            bar()
            v_(lambda e: e.tensor_scalar(out=WSa[:, 2048:3072], in0=WSa[:, 2048:3072], scalar1=0.125,
                                         scalar2=0.125, op0=OP.mult, op1=OP.add))
            bar()
            v_(lambda e: e.scalar_tensor_tensor(out=WSa[:, 2048:3072], in0=WSa[:, 3072:4096],
                                                scalar=0.25, in1=WSa[:, 2048:3072], op0=OP.mult, op1=OP.add))
            bar()
            v_(lambda e: e.tensor_mul(out=WSa[:, 2048:3072].rearrange("p (a b) -> p a b", b=32),
                                      in0=WSa[:, 2048:3072].rearrange("p (a b) -> p a b", b=32),
                                      in1=AP(q_sc, 0, [[32, 128], [1, 32], [0, 32]])))
            bar()
            v_(lambda e, d=dst_bf: e.tensor_mul(out=d, in0=WSa[:, 1024:2048], in1=WSa[:, 2048:3072]))
            bar()

        for ch in range(NC):
            dma("sync", "A", WSb[:, 0:1024], D["l1fw"][ch * 128:(ch + 1) * 128, :])
            bar()
            quantize(GB[:, 8192 + ch * 1024:8192 + (ch + 1) * 1024])
        for oc in range(NC):
            for grp in range(2):
                for k in range(4):
                    t_(lambda e, d=grp * 4 + k, o=oc, kk=k: e.matmul(
                        out=PA[:, kk * 128:(kk + 1) * 128],
                        lhsT=GB[:, 8192 + d * 1024 + o * 128:8192 + d * 1024 + (o + 1) * 128],
                        rhs=idbf[:], start=True, stop=True))
                bar()
                for k in range(4):
                    s_(lambda e, d=grp * 4 + k, o=oc, kk=k: e.copy(
                        out=GB[:, o * 1024 + d * 128:o * 1024 + (d + 1) * 128],
                        in_=PA[:, kk * 128:(kk + 1) * 128]))
                bar()
        for ti in range(NT):
            layer_norm_tile(xscr, ti, xnt[:], out_f32_ap=WSb[:, 0:1024])
            quantize(GA[:, ti * 1024:(ti + 1) * 1024])
        for ti in range(NT):
            for grp in range(2):
                for k in range(4):
                    t_(lambda e, c=grp * 4 + k, t=ti, kk=k: e.matmul(
                        out=PA[:, kk * 128:(kk + 1) * 128],
                        lhsT=GA[:, t * 1024 + c * 128:t * 1024 + (c + 1) * 128],
                        rhs=idbf[:], start=True, stop=True))
                bar()
                for k in range(4):
                    s_(lambda e, c=grp * 4 + k, t=ti, kk=k: e.copy(
                        out=GC[:, c * 2048 + t * 128:c * 2048 + (t + 1) * 128],
                        in_=PA[:, kk * 128:(kk + 1) * 128]))
                bar()
        for ti in range(NT):
            ps = PA if ti % 2 == 0 else PB
            xa = xb[ti % 2]
            dma("gpsimd", "B", xa[:], xscr[ti * 128:(ti + 1) * 128, :])
            for kc in range(NC):
                for nh2 in range(2):
                    t_(lambda e, k=kc, n=nh2, p=ps, t=ti: e.matmul(
                        out=p[:, n * 512:(n + 1) * 512],
                        lhsT=GC[:, k * 2048 + t * 128:k * 2048 + (t + 1) * 128],
                        rhs=GB[:, k * 1024 + n * 512:k * 1024 + (n + 1) * 512],
                        start=(k == 0), stop=False))
            bar()
            for nh2 in range(2):
                t_(lambda e, n=nh2, p=ps: e.matmul(
                    out=p[:, n * 512:(n + 1) * 512], lhsT=ones512[0:1, 0:128],
                    rhs=l1fbrow[0:1, n * 512:(n + 1) * 512], start=False, stop=True))
            bar()
            v_(lambda e, p=ps, a=xa: e.tensor_add(out=a[:], in0=a[:], in1=p[:, 0:1024]))
            bar()
            dma("gpsimd", "B", xout[ti * 128:(ti + 1) * 128, :], xa[:])
        bar()
        prog.on("sync", lambda e: e.wait_ge(dA, 16 * prog.dma_issued["A"]))
        prog.on("sync", lambda e: e.wait_ge(dB, 16 * prog.dma_issued["B"]))

        emit(prog, block, bsem, dsems)
    return nc


def host_prepare(inp, core):
    import ml_dtypes
    d = {}
    d["x_in"] = np.ascontiguousarray(inp["L0_attn_x"][core])
    d["wkT"] = np.ascontiguousarray(inp["L0_attn_Wk"].T)
    d["wvT"] = np.ascontiguousarray(inp["L0_attn_Wv"].T)
    d["wrT"] = np.ascontiguousarray(inp["L0_attn_Wr"].T)
    d["woT"] = np.ascontiguousarray(inp["L0_attn_Wo"].T)
    colv = np.zeros((128, 40), np.float32)
    colv[:, 0:8] = inp["L0_attn_time_decay"].reshape(8, 128).T
    colv[:, 8:16] = inp["L0_attn_time_first"].reshape(8, 128).T
    colv[:, 16:24] = inp["L0_attn_mix_k"].reshape(8, 128).T
    colv[:, 24:32] = inp["L0_attn_mix_v"].reshape(8, 128).T
    colv[:, 32:40] = inp["L0_attn_mix_r"].reshape(8, 128).T
    d["colv"] = colv
    d["wexp"] = np.ascontiguousarray(inp["L0_ffn_W_exp"])
    fp2f = np.empty(HID, np.int64)
    for fp in range(HID):
        fp2f[fp] = (fp % 32) * 128 + fp // 32
    d["wconP"] = np.ascontiguousarray(inp["L0_ffn_W_con"][fp2f])
    d["becol"] = np.ascontiguousarray(inp["L0_ffn_b_exp"].reshape(32, 128).T)
    d["ubPcol"] = np.ascontiguousarray(inp["L0_ffn_up_bias"][fp2f].reshape(32, 128).T)

    bf_init = np.zeros((4, 128, 1024), np.float32)
    bf_wex = np.zeros((4, 4, 4, 128, 512), np.float32)
    for bi, wkey in enumerate(["L0_ffn_up_weights", "L0_ffn_down_weights"]):
        W = inp[wkey]

        def wval(st, f, h1, h2):
            s = 2 ** st
            k = (f // (2 * s)) * s + f % s
            return W[st, k, h1, h2]
        ini = np.zeros((128, 32, 32), np.float32)
        ar = np.arange(32)
        for p in range(128):
            f = p * 32 + ar
            h = ar % 2
            for a in range(32):
                ini[p, a, a] = wval(0, f[a], h[a], h[a])
                ini[p, a, a ^ 1] = wval(0, f[a], h[a], 1 - h[a])
        bf_init[bi * 2] = ini.reshape(128, 1024)
        for si in range(4):
            st = si + 1
            s = 2 ** st
            wex = np.zeros((4, 128, 16, 32), np.float32)
            for p in range(128):
                for am in range(16):
                    b2, pos = am // s, am % s
                    for h in range(2):
                        a = b2 * 2 * s + h * s + pos
                        f = p * 32 + a
                        wd = wval(st, f, h, h)
                        wo = wval(st, f, h, 1 - h)
                        if h == 0:
                            wex[0, p, am, :] = wd
                            wex[1, p, am, :] = wo
                        else:
                            wex[3, p, am, :] = wd
                            wex[2, p, am, :] = wo
            bf_wex[bi * 2, si] = wex.reshape(4, 128, 512)
        def wvalB(st, cpos, l, h1, h2):
            f = cpos * 128 + l
            s = 2 ** st
            k = (f // (2 * s)) * s + f % s
            return W[st, k, h1, h2]
        ini = np.zeros((128, 32, 32), np.float32)
        sj = 16
        for p in range(128):
            for a in range(32):
                h = (a // sj) % 2
                ini[p, a, a] = wvalB(11, a, p, h, h)
                ini[p, a, a ^ sj] = wvalB(11, a ^ sj, p, h ^ 1, h) if False else wvalB(11, a, p, 1 - h, h)
        bf_init[bi * 2 + 1] = ini.reshape(128, 1024)
        for si in range(4):
            st = 10 - si
            sj = 2 ** (st - 7)
            wex = np.zeros((4, 128, 16, 32), np.float32)
            for p in range(128):
                for am in range(16):
                    b2, pos = am // sj, am % sj
                    for h in range(2):
                        a = b2 * 2 * sj + h * sj + pos
                        wd = wvalB(st, a, p, h, h)
                        wo = wvalB(st, a, p, 1 - h, h)
                        if h == 0:
                            wex[0, p, am, :] = wd
                            wex[1, p, am, :] = wo
                        else:
                            wex[3, p, am, :] = wd
                            wex[2, p, am, :] = wo
            bf_wex[bi * 2 + 1, si] = wex.reshape(4, 128, 512)
    d["bf_init"] = bf_init
    d["bf_wex"] = bf_wex
    bf56_d = np.zeros((2, 2, 128, 32), np.float32)
    bf56_o = np.zeros((2, 2, 128, 32), np.float32)
    for bi, wkey in enumerate(["L0_ffn_up_weights", "L0_ffn_down_weights"]):
        W = inp[wkey]
        for sti, st in enumerate((5, 6)):
            s = 2 ** st
            for p in range(128):
                for c in range(32):
                    f = c * 128 + p
                    k = (f // (2 * s)) * s + f % s
                    h = (f // s) % 2
                    bf56_d[bi, sti, p, c] = W[st, k, h, h]
                    bf56_o[bi, sti, p, c] = W[st, k, h, 1 - h]
    d["bf56_d"] = bf56_d
    d["bf56_o"] = bf56_o
    d["l1wq"] = np.ascontiguousarray(inp["L1_attn_Wq"])
    d["l1wk"] = np.ascontiguousarray(inp["L1_attn_Wk"])
    d["l1wv"] = np.ascontiguousarray(inp["L1_attn_Wv"])
    d["l1wo"] = np.ascontiguousarray(inp["L1_attn_Wo"])
    l1b = np.zeros((128, 16), np.float32)
    l1b[:, 0:8] = inp["L1_attn_bq"].reshape(8, 128).T
    l1b[:, 8:16] = inp["L1_attn_bk"].reshape(8, 128).T
    d["l1bcol"] = l1b
    brows = np.zeros((1, 3072), np.float32)
    brows[0, 0:1024] = inp["L1_attn_bv"]
    brows[0, 1024:2048] = inp["L1_attn_bo"]
    brows[0, 2048:3072] = inp["L0_ffn_b_con"]
    d["brows"] = brows
    d["dbrow"] = inp["L0_ffn_down_bias"].reshape(1, C)
    d["l1fbrow"] = inp["L1_ffn_bias"].reshape(1, C)
    d["l1fw"] = np.ascontiguousarray(inp["L1_ffn_weight"])
    d["c_idbf"] = np.eye(128, dtype=ml_dtypes.bfloat16)
    d["c_idf"] = np.eye(128, dtype=np.float32)
    cm = np.zeros((4, 128, 512), np.float32)
    for dd in range(4):
        p = np.arange(128)[:, None]
        ql = np.arange(512)[None, :]
        cm[dd] = np.where(p + dd * 128 > ql, NEG, 0.0)
    d["c_mask"] = cm.astype(ml_dtypes.bfloat16)
    pi = np.zeros((2, 128, 128), np.float32)
    for sti, s in enumerate((32, 64)):
        for i in range(128):
            pi[sti, i, i ^ s] = 1.0
    d["c_pi"] = pi
    ind = np.zeros((2, 128), np.float32)
    ind[0, 0:64] = 1.0
    ind[1, 64:128] = 1.0
    d["c_ind2"] = ind
    d["c_ones512"] = np.ones((1, 512), ml_dtypes.bfloat16)
    d["c_onesf"] = np.ones((1, 128), np.float32)
    return d


def kernel(**inputs):
    nc = build()
    in_maps = [host_prepare(inputs, core) for core in range(8)]
    res = run_bass_kernel_spmd(nc, in_maps, core_ids=list(range(8)))
    out = np.stack([r["x_out"] for r in res.results], axis=0)
    return out.astype(np.float32)


# version 10 (final)
# speedup vs baseline: 1.1334x; 1.1334x over previous
import numpy as np
import contextlib
from collections import defaultdict

import concourse.bass as bass
import concourse.mybir as mybir
from concourse.bass_utils import run_bass_kernel_spmd

F32 = mybir.dt.float32
BF16 = mybir.dt.bfloat16
AF = mybir.ActivationFunctionType
OP = mybir.AluOpType
AX = mybir.AxisListType
AP = bass.AP

T, C, HID = 2048, 1024, 4096
NT, NC, NH = 16, 8, 32
NHEAD, HD = 16, 64
NEG = -1.0e9
MAGIC = 12582912.0

ENGS = ("tensor", "vector", "scalar", "gpsimd", "sync")


class Prog:
    def __init__(self):
        self.phases = [defaultdict(list)]
        self.dma_issued = defaultdict(int)
        self.dma_at_phase = []

    def on(self, eng, fn):
        self.phases[-1][eng].append(fn)

    def dma(self, eng, q, dst, src):
        self.dma_issued[q] += 1
        self.phases[-1][eng].append(
            ("dma", q, lambda e, s, d=dst, sc=src: e.dma_start(out=d, in_=sc).then_inc(s, 16)))

    def barrier(self):
        self.dma_at_phase.append(dict(self.dma_issued))
        self.phases.append(defaultdict(list))


def emit(prog, block, bsem, dsems):
    prog.barrier()
    nph = len(prog.dma_at_phase)

    def has_nondma(p, e):
        return any(not isinstance(op, tuple) for op in prog.phases[p][e])

    incs = [sum(1 for e in ENGS if has_nondma(p, e)) for p in range(nph)]
    btarget = [0] * (nph + 1)
    for p in range(nph):
        btarget[p + 1] = btarget[p] + incs[p]

    def make_body(ename):
        def body(eng):
            for p in range(nph):
                ops = prog.phases[p][ename]
                if not ops:
                    continue
                if p > 0:
                    eng.wait_ge(bsem, btarget[p])
                    prev = prog.dma_at_phase[p - 1]
                    for q, sem in dsems.items():
                        if prev.get(q, 0) > 0:
                            eng.wait_ge(sem, 16 * prev[q])
                last = None
                for op in ops:
                    if isinstance(op, tuple):
                        _, q, fn = op
                        fn(eng, dsems[q])
                    else:
                        r = op(eng)
                        if r is not None:
                            last = r
                if p < nph - 1 and has_nondma(p, ename):
                    assert last is not None, f"phase {p} engine {ename}: no instr"
                    last.then_inc(bsem, 1)
        return body

    for ename in ENGS:
        getattr(block, ename)(make_body(ename))


HOSTIN = [
    ("x_in", [T, C], F32), ("wkT", [C, C], F32), ("wvT", [C, C], F32),
    ("wrT", [C, C], F32), ("woT", [C, C], F32),
    ("colv", [128, 40], F32), ("wexp", [C, HID], F32), ("wconP", [HID, C], F32),
    ("becol", [128, 32], F32), ("ubPcol", [128, 32], F32),
    ("bf_init", [4, 128, 1024], F32),
    ("bf_wex", [4, 4, 4, 128, 512], F32),
    ("bf56_d", [2, 2, 128, 32], F32), ("bf56_o", [2, 2, 128, 32], F32),
    ("l1wq", [C, C], F32), ("l1wk", [C, C], F32), ("l1wv", [C, C], F32), ("l1wo", [C, C], F32),
    ("l1bcol", [128, 16], F32),
    ("brows", [1, 3072], F32),
    ("dbrow", [1, C], F32),
    ("l1fbrow", [1, C], F32), ("l1fw", [C, C], F32),
    ("c_idbf", [128, 128], BF16), ("c_idf", [128, 128], F32),
    ("c_mask", [4, 128, 512], BF16), ("c_pi", [2, 128, 128], F32),
    ("c_ind2", [2, 128], F32),
    ("c_ones512", [1, 512], BF16), ("c_onesf", [1, 128], F32),
]


def build():
    nc = bass.Bass("TRN2", target_bir_lowering=False, debug=False)
    D = {}
    for nm, shp, dt in HOSTIN:
        D[nm] = nc.dram_tensor(nm, shp, dt, kind="ExternalInput").ap()
    xout = nc.dram_tensor("x_out", [T, C], F32, kind="ExternalOutput").ap()
    tap0 = nc.dram_tensor("tap0", [T, C], F32, kind="ExternalOutput").ap()
    tap1 = nc.dram_tensor("tap1", [T, C], F32, kind="ExternalOutput").ap()
    tap2 = nc.dram_tensor("tap2", [T, C], F32, kind="ExternalOutput").ap()
    dbg = nc.dram_tensor("dbg", [128, 2048], F32, kind="ExternalOutput").ap()

    xscr = nc.dram_tensor("xscr", [T, C], F32, kind="Internal").ap()
    wexp_bf = nc.dram_tensor("wexp_bf", [C * HID], BF16, kind="Internal").ap()
    wconP_bf = nc.dram_tensor("wconP_bf", [HID * C], BF16, kind="Internal").ap()
    smscr = nc.dram_tensor("smscr", [128 * 1024], F32, kind="Internal").ap()
    hscr = nc.dram_tensor("hscr", [NH * 128 * 512], BF16, kind="Internal").ap()
    denscr = nc.dram_tensor("denscr", [NHEAD * T], F32, kind="Internal").ap()
    mscr = nc.dram_tensor("mscr", [32], F32, kind="Internal").ap()

    es = contextlib.ExitStack()
    prog = Prog()
    with es:
        def sbt(nm, shp, dt=F32):
            return es.enter_context(nc.sbuf_tensor(nm, shp, dt))

        idbf = sbt("idbf", [128, 128], BF16)
        idf = sbt("idf", [128, 128], F32)
        maskd = sbt("maskd", [128, 4 * 512], BF16)
        pim = sbt("pim", [128, 256], F32)
        ind2 = sbt("ind2", [2, 128], F32)
        ones512 = sbt("ones512", [1, 512], BF16)
        onesf = sbt("onesf", [1, 128], F32)
        colv = sbt("colv_s", [128, 40], F32)
        col56 = sbt("col56", [128, 64], F32)
        becol = sbt("becol_s", [128, 32], F32)
        ubPcol = sbt("ubPcol_s", [128, 32], F32)
        l1bcol = sbt("l1bcol_s", [128, 16], F32)
        brows = sbt("brows_s", [1, 3072], BF16)
        l1fbrow = sbt("l1fbrow_s", [1, 1024], BF16)
        stat = sbt("stat", [128, 8], F32)
        stat2 = sbt("stat2", [16, 4], F32)
        carryN = sbt("carryN", [128, 8], F32)
        carryD = sbt("carryD", [128, 8], F32)
        prevc = sbt("prevc", [128, 8], BF16)
        mm8 = sbt("mm8", [128, 16], F32)
        m82 = sbt("m82", [16, 128], F32)
        negM = sbt("negM", [128, 16], F32)
        den2 = sbt("den2", [2, 2048], F32)
        mrow = sbt("mrow", [1, 32], F32)
        q_sc = sbt("q_sc", [128, 32], F32)
        q_isc = sbt("q_isc", [128, 32], F32)

        GA = sbt("GA", [128, 16384], BF16)
        GB = sbt("GB", [128, 16384], BF16)
        GC = sbt("GC", [128, 16384], BF16)
        GD = sbt("GD", [128, 16384], BF16)
        vtmx = sbt("vtmx", [128, 1040], BF16)
        AAF1 = sbt("AAF1", [128, 4096], F32)
        AAF2 = sbt("AAF2", [128, 4096], F32)
        WB1 = sbt("WB1", [128, 1024], BF16)
        WB2 = sbt("WB2", [128, 1024], BF16)
        ptile = sbt("ptile", [128, 2048], BF16)
        xb = [sbt(f"xb{i}", [128, 1024], F32) for i in range(2)]
        xnt = sbt("xnt", [128, 1024], BF16)

        PA = es.enter_context(nc.psum_tensor("PA", [128, 1024], F32))
        PB = es.enter_context(nc.psum_tensor("PB", [128, 1024], F32))
        PC = es.enter_context(nc.psum_tensor("PC", [128, 512], F32))
        PD = es.enter_context(nc.psum_tensor("PD", [128, 512], F32))
        PE = es.enter_context(nc.psum_tensor("PE", [128, 512], F32))
        PF = es.enter_context(nc.psum_tensor("PF", [128, 512], F32))
        PSLOT = ([(PA, i * 128) for i in range(8)] + [(PB, i * 128) for i in range(8)] +
                 [(PC, i * 128) for i in range(4)] + [(PD, i * 128) for i in range(4)] +
                 [(PE, i * 128) for i in range(4)] + [(PF, i * 128) for i in range(4)])

        bsem = es.enter_context(nc.semaphore("bsem"))
        dA = es.enter_context(nc.semaphore("dA"))
        dB = es.enter_context(nc.semaphore("dB"))
        dsems = {"A": dA, "B": dB}
        block = es.enter_context(nc.Block())

        bar = prog.barrier

        def t_(fn):
            prog.on("tensor", fn)

        def v_(fn):
            prog.on("vector", fn)

        def s_(fn):
            prog.on("scalar", fn)

        def g_(fn):
            prog.on("gpsimd", fn)

        dma = prog.dma
        WSa = AAF1
        WSb = AAF2

        dma("sync", "A", idbf[:], D["c_idbf"])
        dma("sync", "A", idf[:], D["c_idf"])
        for d in range(4):
            dma("sync", "A", maskd[:, d * 512:(d + 1) * 512], D["c_mask"][d])
        dma("sync", "A", pim[:, 0:128], D["c_pi"][0])
        dma("sync", "A", pim[:, 128:256], D["c_pi"][1])
        dma("sync", "A", ind2[:], D["c_ind2"])
        dma("sync", "A", ones512[:], D["c_ones512"])
        dma("sync", "A", onesf[:], D["c_onesf"])
        dma("sync", "A", colv[:], D["colv"])
        dma("sync", "A", becol[:], D["becol"])
        dma("sync", "A", ubPcol[:], D["ubPcol"])
        dma("sync", "A", l1bcol[:], D["l1bcol"])
        dma("sync", "A", WSa[0:1, 0:3072], D["brows"])
        dma("sync", "A", WSb[0:1, 0:1024], D["dbrow"])
        dma("sync", "A", WSb[0:1, 1024:2048], D["l1fbrow"])
        bar()
        s_(lambda e: e.activation(out=colv[:, 0:8], in_=colv[:, 0:8], func=AF.Exp, bias=0.0, scale=1.0))
        v_(lambda e: e.tensor_add(out=WSa[0:1, 2048:3072], in0=WSa[0:1, 2048:3072], in1=WSb[0:1, 0:1024]))
        bar()
        s_(lambda e: e.activation(out=colv[:, 0:8], in_=colv[:, 0:8], func=AF.Exp, bias=0.0, scale=-1.0))
        s_(lambda e: e.activation(out=colv[:, 8:16], in_=colv[:, 8:16], func=AF.Exp, bias=0.0, scale=1.0))
        v_(lambda e: e.tensor_copy(out=brows[0:1, :], in_=WSa[0:1, 0:3072]))
        bar()
        v_(lambda e: e.tensor_copy(out=l1fbrow[0:1, :], in_=WSb[0:1, 1024:2048]))
        bar()

        for i in range(16):
            r, cp = divmod(i, 2)
            buf = WSa[:, 0:2048] if i % 2 == 0 else WSa[:, 2048:4096]
            bfb = GA[:, 0:2048] if i % 2 == 0 else GA[:, 2048:4096]
            dma("sync", "A", buf, D["wexp"][r * 128:(r + 1) * 128, cp * 2048:(cp + 1) * 2048])
            bar()
            v_(lambda e, b=buf, o=bfb: e.tensor_copy(out=o, in_=b))
            bar()
            dma("gpsimd", "B", AP(wexp_bf.tensor, r * 128 * 4096 + cp * 2048,
                                  [[4096, 128], [1, 2048]]), bfb)
        for i in range(16):
            buf = WSa[:, 0:2048] if i % 2 == 0 else WSa[:, 2048:4096]
            bfb = GA[:, 0:2048] if i % 2 == 0 else GA[:, 2048:4096]
            dma("sync", "A", buf, AP(D["wconP"].tensor, i * 256 * 1024,
                                     [[1024, 128], [128 * 1024, 2], [1, 1024]]))
            bar()
            v_(lambda e, b=buf, o=bfb: e.tensor_copy(out=o, in_=b))
            bar()
            dma("gpsimd", "B", AP(wconP_bf.tensor, i * 256 * 1024, [[2048, 128], [1, 2048]]), bfb)
        bar()

        def layer_norm_tile(xsrc, ti, out_bf_ap, out_f32_ap=None):
            xa = xb[ti % 2]
            dma("sync", "A", xa[:], xsrc[ti * 128:(ti + 1) * 128, :])
            bar()
            v_(lambda e, a=xa: e.reduce_sum(out=stat[:, 0:1], in_=a[:], axis=AX.X))
            s_(lambda e, a=xa: e.activation(out=WSb[:, 3072:4096], in_=a[:], func=AF.Square,
                                            bias=0.0, scale=1.0, accum_out=stat[:, 1:2]))
            bar()
            v_(lambda e: e.tensor_scalar(out=stat[:, 2:3], in0=stat[:, 0:1], scalar1=1.0 / C,
                                         scalar2=None, op0=OP.mult))
            bar()
            v_(lambda e: e.tensor_mul(out=stat[:, 3:4], in0=stat[:, 2:3], in1=stat[:, 2:3]))
            bar()
            v_(lambda e: e.tensor_scalar(out=stat[:, 4:5], in0=stat[:, 1:2], scalar1=1.0 / C,
                                         scalar2=stat[:, 3:4], op0=OP.mult, op1=OP.subtract))
            bar()
            v_(lambda e: e.tensor_single_scalar(out=stat[:, 4:5], in_=stat[:, 4:5],
                                                scalar=1.0e-5, op=OP.add))
            bar()
            s_(lambda e: e.activation(out=stat[:, 5:6], in_=stat[:, 4:5], func=AF.Sqrt,
                                      bias=0.0, scale=1.0))
            bar()
            v_(lambda e: e.reciprocal(out=stat[:, 6:7], in_=stat[:, 5:6]))
            bar()
            v_(lambda e, a=xa, o=out_bf_ap: e.tensor_scalar(out=o, in0=a[:], scalar1=stat[:, 2:3],
                                                            scalar2=stat[:, 6:7], op0=OP.subtract, op1=OP.mult))
            if out_f32_ap is not None:
                v_(lambda e, a=xa, o=out_f32_ap: e.tensor_scalar(out=o, in0=a[:], scalar1=stat[:, 2:3],
                                                                 scalar2=stat[:, 6:7], op0=OP.subtract, op1=OP.mult))
            bar()

        def tile_to_fm(src_tile, dst_fn):
            for grp in range(2):
                for k in range(4):
                    t_(lambda e, c=grp * 4 + k, kk=k, st=src_tile: e.matmul(
                        out=PA[:, kk * 128:(kk + 1) * 128],
                        lhsT=st[:, c * 128:(c + 1) * 128], rhs=idbf[:], start=True, stop=True))
                bar()
                for k in range(4):
                    s_(lambda e, c=grp * 4 + k, kk=k, df=dst_fn: e.copy(
                        out=df(c), in_=PA[:, kk * 128:(kk + 1) * 128]))
                bar()

        def load_cast(dst_bf, src_f32_ap):
            dma("sync", "A", WSb[:, 2048:3072], src_f32_ap)
            bar()
            v_(lambda e, d=dst_bf: e.tensor_copy(out=d, in_=WSb[:, 2048:3072]))
            bar()

        def xnv(cc):
            return GA[:, cc * 1024:(cc + 1) * 1024]

        def mixv(cc):
            return GA[:, 8192 + cc * 1024: 8192 + (cc + 1) * 1024]

        def ekv(cc):
            return GB[:, cc * 1024:(cc + 1) * 1024]

        def vv(cc):
            return GB[:, 8192 + cc * 1024: 8192 + (cc + 1) * 1024]

        def sgv(cc):
            return GC[:, cc * 1024:(cc + 1) * 1024]

        def rwv(cc):
            return GC[:, 8192 + cc * 1024: 8192 + (cc + 1) * 1024]

        for half in range(2):
            toff = half * 8
            for ti in range(8):
                layer_norm_tile(D["x_in"], toff + ti, xnt[:])
                tile_to_fm(xnt, lambda c, t=ti: xnv(c)[:, t * 128:(t + 1) * 128])
            for pi, (wname, mixcol) in enumerate([("wkT", 16), ("wvT", 24), ("wrT", 32)]):
                for cc in range(NC):
                    v_(lambda e, c=cc: e.tensor_sub(out=WSa[:, 0:1023], in0=xnv(c)[:, 1:1024],
                                                    in1=xnv(c)[:, 0:1023]))
                    bar()
                    v_(lambda e, c=cc, m=mixcol: e.scalar_tensor_tensor(
                        out=mixv(c)[:, 1:1024], in0=WSa[:, 0:1023], scalar=colv[:, m + c:m + c + 1],
                        in1=xnv(c)[:, 0:1023], op0=OP.mult, op1=OP.add))
                    if half == 0:
                        v_(lambda e, c=cc, m=mixcol: e.tensor_scalar(
                            out=mixv(c)[:, 0:1], in0=xnv(c)[:, 0:1], scalar1=colv[:, m + c:m + c + 1],
                            scalar2=None, op0=OP.mult))
                        bar()
                    else:
                        v_(lambda e, c=cc: e.tensor_copy(out=WSa[:, 1100:1101], in_=prevc[:, c:c + 1]))
                        bar()
                        v_(lambda e, c=cc: e.tensor_sub(out=WSa[:, 1101:1102], in0=xnv(c)[:, 0:1],
                                                        in1=WSa[:, 1100:1101]))
                        bar()
                        v_(lambda e, c=cc, m=mixcol: e.scalar_tensor_tensor(
                            out=mixv(c)[:, 0:1], in0=WSa[:, 1101:1102], scalar=colv[:, m + c:m + c + 1],
                            in1=WSa[:, 1100:1101], op0=OP.mult, op1=OP.add))
                        bar()
                for oc in range(NC):
                    wb = WB1 if oc % 2 == 0 else WB2
                    load_cast(wb[:], AP(D[wname].tensor, oc * 128, [[1024, 128], [131072, 8], [1, 128]]))
                    ps = PA if oc % 2 == 0 else PB
                    for kc in range(NC):
                        for nh2 in range(2):
                            t_(lambda e, k=kc, n=nh2, w=wb, p=ps: e.matmul(
                                out=p[:, n * 512:(n + 1) * 512],
                                lhsT=w[:, k * 128:(k + 1) * 128],
                                rhs=mixv(k)[:, n * 512:(n + 1) * 512],
                                start=(k == 0), stop=(k == NC - 1)))
                    bar()
                    if pi == 0:
                        s_(lambda e, o=oc, p=ps: e.activation(out=ekv(o), in_=p[:, 0:1024],
                                                              func=AF.Exp, bias=0.0, scale=1.0))
                    elif pi == 1:
                        s_(lambda e, o=oc, p=ps: e.copy(out=vv(o), in_=p[:, 0:1024]))
                    else:
                        s_(lambda e, o=oc, p=ps: e.activation(out=sgv(o), in_=p[:, 0:1024],
                                                              func=AF.Sigmoid, bias=0.0, scale=1.0))
                    bar()
            if half == 0:
                for cc in range(NC):
                    v_(lambda e, c=cc: e.tensor_copy(out=prevc[:, c:c + 1], in_=xnv(c)[:, 1023:1024]))
                bar()
            for cc in range(NC):
                ewb = AP(colv, cc, [[40, 128], [0, 1024]])
                v_(lambda e, c=cc: e.tensor_mul(out=WB1[:], in0=ekv(c), in1=vv(c)))
                bar()
                initN = 0.0 if half == 0 else carryN[:, cc:cc + 1]
                initD = 0.0 if half == 0 else carryD[:, cc:cc + 1]
                v_(lambda e, w=ewb, iN=initN: e.tensor_tensor_scan(
                    out=WSa[:, 0:1024], data0=w, data1=WB1[:], initial=iN, op0=OP.mult, op1=OP.add))
                v_(lambda e, w=ewb, c=cc, iD=initD: e.tensor_tensor_scan(
                    out=WSb[:, 0:1024], data0=w, data1=ekv(c), initial=iD, op0=OP.mult, op1=OP.add))
                bar()
                v_(lambda e, c=cc: e.tensor_scalar(out=WSa[:, 2048:3072], in0=ekv(c),
                                                   scalar1=colv[:, 8 + c:8 + c + 1], scalar2=None, op0=OP.mult))
                bar()
                v_(lambda e, c=cc: e.tensor_mul(out=WSa[:, 3072:4096], in0=WSa[:, 2048:3072], in1=vv(c)))
                v_(lambda e: e.tensor_add(out=WSb[:, 1025:2048], in0=WSa[:, 2049:3072], in1=WSb[:, 0:1023]))
                bar()
                v_(lambda e: e.tensor_add(out=WSa[:, 1025:2048], in0=WSa[:, 3073:4096], in1=WSa[:, 0:1023]))
                if half == 0:
                    v_(lambda e: e.tensor_copy(out=WSb[:, 1024:1025], in_=WSa[:, 2048:2049]))
                    bar()
                    v_(lambda e: e.tensor_copy(out=WSa[:, 1024:1025], in_=WSa[:, 3072:3073]))
                else:
                    v_(lambda e, c=cc: e.tensor_add(out=WSb[:, 1024:1025], in0=WSa[:, 2048:2049],
                                                    in1=carryD[:, c:c + 1]))
                    bar()
                    v_(lambda e, c=cc: e.tensor_add(out=WSa[:, 1024:1025], in0=WSa[:, 3072:3073],
                                                    in1=carryN[:, c:c + 1]))
                bar()
                v_(lambda e, c=cc: e.tensor_copy(out=carryN[:, c:c + 1], in_=WSa[:, 1023:1024]))
                v_(lambda e, c=cc: e.tensor_copy(out=carryD[:, c:c + 1], in_=WSb[:, 1023:1024]))
                v_(lambda e: e.reciprocal(out=WSb[:, 1024:2048], in_=WSb[:, 1024:2048]))
                bar()
                v_(lambda e: e.tensor_mul(out=WSa[:, 1024:2048], in0=WSa[:, 1024:2048], in1=WSb[:, 1024:2048]))
                bar()
                v_(lambda e, c=cc: e.tensor_mul(out=rwv(c), in0=WSa[:, 1024:2048], in1=sgv(c)))
                bar()
            if half == 0:
                for kc in range(NC):
                    load_cast(GD[:, kc * 1024:(kc + 1) * 1024], D["woT"][kc * 128:(kc + 1) * 128, :])
            for ti in range(8):
                ps = PA if ti % 2 == 0 else PB
                xa = xb[ti % 2]
                dma("gpsimd", "B", xa[:], D["x_in"][(toff + ti) * 128:(toff + ti + 1) * 128, :])
                for kc in range(NC):
                    for nh2 in range(2):
                        t_(lambda e, k=kc, n=nh2, p=ps, t=ti: e.matmul(
                            out=p[:, n * 512:(n + 1) * 512],
                            lhsT=rwv(k)[:, t * 128:(t + 1) * 128],
                            rhs=GD[:, k * 1024 + n * 512:k * 1024 + (n + 1) * 512],
                            start=(k == 0), stop=(k == NC - 1)))
                bar()
                v_(lambda e, p=ps, a=xa: e.tensor_add(out=a[:], in0=a[:], in1=p[:, 0:1024]))
                bar()
                dma("gpsimd", "B", xscr[(toff + ti) * 128:(toff + ti + 1) * 128, :], xa[:])
            bar()

        dma("sync", "A", tap0, xscr)
        bar()
        for bi in range(2):
            for part in range(2):
                pidx = bi * 2 + part
                dma("sync", "A", WSa[:, 0:1024], D["bf_init"][pidx])
                bar()
                bufs = [WSa, WSb]
                cur = 0
                for si in range(4):
                    s = 2 ** (si + 1) if part == 0 else 2 ** (3 - si)
                    nb2 = 16 // s
                    wext = bufs[1 - cur]
                    for w4 in range(4):
                        dma("sync", "A", wext[:, 1024 + w4 * 512:1024 + (w4 + 1) * 512],
                            D["bf_wex"][pidx, si, w4])
                    bar()
                    cb, nb = bufs[cur], bufs[1 - cur]
                    for h in range(2):
                        wd = 0 if h == 0 else 3
                        wo = 1 if h == 0 else 2
                        v_(lambda e, hh=h, w=wd, cbuf=cb, nbuf=nb, wx=wext, ss=s, nn=nb2: e.tensor_mul(
                            out=AP(nbuf, hh * ss * 32, [[4096, 128], [64 * ss, nn], [1, 32 * ss]]),
                            in0=AP(cbuf, hh * ss * 32, [[4096, 128], [64 * ss, nn], [1, 32 * ss]]),
                            in1=AP(wx, 1024 + w * 512, [[4096, 128], [32 * ss, nn], [1, 32 * ss]])))
                        bar()
                        v_(lambda e, hh=h, w=wo, cbuf=cb, nbuf=nb, wx=wext, ss=s, nn=nb2: e.tensor_mul(
                            out=AP(nbuf, 3072, [[4096, 128], [64 * ss, nn], [1, 32 * ss]]),
                            in0=AP(cbuf, (1 - hh) * ss * 32, [[4096, 128], [64 * ss, nn], [1, 32 * ss]]),
                            in1=AP(wx, 1024 + w * 512, [[4096, 128], [32 * ss, nn], [1, 32 * ss]])))
                        bar()
                        v_(lambda e, hh=h, nbuf=nb, ss=s, nn=nb2: e.tensor_add(
                            out=AP(nbuf, hh * ss * 32, [[4096, 128], [64 * ss, nn], [1, 32 * ss]]),
                            in0=AP(nbuf, hh * ss * 32, [[4096, 128], [64 * ss, nn], [1, 32 * ss]]),
                            in1=AP(nbuf, 3072, [[4096, 128], [64 * ss, nn], [1, 32 * ss]])))
                        bar()
                    cur = 1 - cur
                dma("gpsimd", "B", AP(smscr.tensor, 0, [[1024, 128], [1, 1024]]), bufs[cur][:, 0:1024])
                bar()
                g_(lambda e: e.memset(AAF2[:, 0:4096], 0.0))
                bar()
                for q in range(4):
                    dma("sync", "A",
                        AP(AAF2, q * 32 * 4096 + q * 32, [[4096, 32], [128, 32], [1, 32]]),
                        AP(smscr.tensor, q * 1024, [[32, 32], [4096, 32], [1, 32]]))
                bar()
                if part == 0:
                    for st in range(2):
                        dma("sync", "A", col56[:, 0:32], D["bf56_d"][bi, st])
                        dma("sync", "A", col56[:, 32:64], D["bf56_o"][bi, st])
                        bar()
                        for ch in range(32):
                            pp, off = PSLOT[ch]
                            t_(lambda e, c=ch, p=pp, o=off, s2=st: e.matmul(
                                out=p[:, o:o + 128], lhsT=pim[:, s2 * 128:(s2 + 1) * 128],
                                rhs=AAF2[:, c * 128:(c + 1) * 128], start=True, stop=True))
                        bar()
                        v_(lambda e: e.tensor_mul(
                            out=AAF1[:, 0:4096].rearrange("p (c k) -> p c k", k=128),
                            in0=AAF2[:, 0:4096].rearrange("p (c k) -> p c k", k=128),
                            in1=AP(col56, 0, [[64, 128], [1, 32], [0, 128]])))
                        bar()
                        pieces = [(PA, 0, 8), (PB, 8, 8), (PC, 16, 4), (PD, 20, 4), (PE, 24, 4), (PF, 28, 4)]
                        for pt, c0, nch in pieces:
                            v_(lambda e, p=pt, cc0=c0, n=nch: e.tensor_mul(
                                out=AAF2[:, cc0 * 128:(cc0 + n) * 128].rearrange("p (c k) -> p c k", k=128),
                                in0=p[:, 0:n * 128].rearrange("p (c k) -> p c k", k=128),
                                in1=AP(col56, 32 + cc0, [[64, 128], [1, n], [0, 128]])))
                        bar()
                        v_(lambda e: e.tensor_add(out=AAF2[:, 0:4096], in0=AAF2[:, 0:4096],
                                                  in1=AAF1[:, 0:4096]))
                        bar()
                    for grp in range(8):
                        for k in range(4):
                            t_(lambda e, c=grp * 4 + k, kk=k: e.matmul(
                                out=PA[:, kk * 128:(kk + 1) * 128],
                                lhsT=AAF2[:, c * 128:(c + 1) * 128], rhs=idf[:], start=True, stop=True))
                        bar()
                        for k in range(4):
                            s_(lambda e, c=grp * 4 + k, kk=k, b=bi: e.copy(
                                out=GD[:, b * 8192 + c * 128:b * 8192 + (c + 1) * 128],
                                in_=PA[:, kk * 128:(kk + 1) * 128]))
                        bar()
                else:
                    v_(lambda e, b=bi: e.tensor_copy(out=GD[:, b * 8192 + 4096:b * 8192 + 8192],
                                                     in_=AAF2[:, 0:4096]))
                    bar()

        def hb1(k):
            return GB[:, k * 512:(k + 1) * 512]

        def hb2(k):
            return GC[:, k * 512:(k + 1) * 512]

        for qt in range(4):
            tt0 = qt * 4
            for ti in range(4):
                layer_norm_tile(xscr, tt0 + ti, xnt[:])
                tile_to_fm(xnt, lambda c, t=ti: GA[:, c * 512 + t * 128: c * 512 + (t + 1) * 128])
            for fc in range(NH):
                wb = WB1 if fc % 2 == 0 else WB2
                dma("sync", "A", wb[:], AP(wexp_bf.tensor, fc * 128, [[4096, 128], [524288, 8], [1, 128]]))
                bar()
                ps = PC if fc % 2 == 0 else PD
                for kc in range(NC):
                    t_(lambda e, k=kc, w=wb, p=ps: e.matmul(
                        out=p[:, 0:512], lhsT=w[:, k * 128:(k + 1) * 128],
                        rhs=GA[:, k * 512:(k + 1) * 512], start=(k == 0), stop=(k == NC - 1)))
                bar()
                s_(lambda e, f=fc, p=ps: e.activation(out=hb1(f), in_=p[:, 0:512], func=AF.Identity,
                                                      bias=becol[:, f:f + 1], scale=1.0))
                bar()
            for fc in range(NH):
                ps = PC if fc % 2 == 0 else PD
                t_(lambda e, f=fc, p=ps: e.matmul(out=p[:, 0:512],
                                                  lhsT=GD[:, f * 128:(f + 1) * 128],
                                                  rhs=hb1(f), start=True, stop=True))
                bar()
                s_(lambda e, f=fc, p=ps: e.copy(out=hb2(f), in_=p[:, 0:512]))
                bar()
            for ck in range(NH):
                dma("gpsimd", "B", AP(hscr.tensor, ck * 65536, [[512, 128], [1, 512]]), hb2(ck))
            bar()
            for kp in range(NH):
                dma("sync", "A", hb1(kp), AP(hscr.tensor, kp * 2048, [[512, 4], [65536, 32], [1, 512]]))
            bar()
            for kp in range(NH):
                ps = PC if kp % 2 == 0 else PD
                t_(lambda e, f=kp, p=ps: e.matmul(out=p[:, 0:512],
                                                  lhsT=GD[:, 4096 + f * 128:4096 + (f + 1) * 128],
                                                  rhs=hb1(f), start=True, stop=True))
                bar()
                s_(lambda e, f=kp, p=ps: e.activation(out=hb2(f), in_=p[:, 0:512], func=AF.Gelu,
                                                      bias=ubPcol[:, f:f + 1], scale=1.0))
                bar()
            for kp in range(NH):
                dma("gpsimd", "B", AP(hscr.tensor, kp * 65536, [[512, 128], [1, 512]]), hb2(kp))
            bar()
            for ck in range(NH):
                dma("sync", "A", hb1(ck), AP(hscr.tensor, ck * 512, [[65536, 32], [16384, 4], [1, 512]]))
            bar()
            for fc in range(NH):
                ps = PC if fc % 2 == 0 else PD
                t_(lambda e, f=fc, p=ps: e.matmul(out=p[:, 0:512],
                                                  lhsT=GD[:, 8192 + f * 128:8192 + (f + 1) * 128],
                                                  rhs=hb1(f), start=True, stop=True))
                bar()
                s_(lambda e, f=fc, p=ps: e.copy(out=hb2(f), in_=p[:, 0:512]))
                bar()
            for ck in range(NH):
                dma("gpsimd", "B", AP(hscr.tensor, ck * 65536, [[512, 128], [1, 512]]), hb2(ck))
            bar()
            for kp in range(NH):
                dma("sync", "A", hb1(kp), AP(hscr.tensor, kp * 2048, [[512, 4], [65536, 32], [1, 512]]))
            bar()
            for kp in range(NH):
                ps = PC if kp % 2 == 0 else PD
                t_(lambda e, f=kp, p=ps: e.matmul(out=p[:, 0:512],
                                                  lhsT=GD[:, 12288 + f * 128:12288 + (f + 1) * 128],
                                                  rhs=hb1(f), start=True, stop=True))
                bar()
                s_(lambda e, f=kp, p=ps: e.copy(out=hb2(f), in_=p[:, 0:512]))
                bar()
            def wcon_psum(ti2, n2):
                if ti2 == 0:
                    return PA, n2 * 512
                if ti2 == 1:
                    return PB, n2 * 512
                if ti2 == 2:
                    return (PC if n2 == 0 else PD), 0
                return (PE if n2 == 0 else PF), 0
            for kp in range(NH):
                wb = WB1 if kp % 2 == 0 else WB2
                dma("sync", "A", wb[:], AP(wconP_bf.tensor, kp * 128 * 1024, [[1024, 128], [1, 1024]]))
                bar()
                for ti in range(4):
                    for n2 in range(2):
                        pp, oo = wcon_psum(ti, n2)
                        t_(lambda e, f=kp, w=wb, p=pp, o=oo, n=n2, t=ti: e.matmul(
                            out=p[:, o:o + 512], lhsT=hb2(f)[:, t * 128:(t + 1) * 128],
                            rhs=w[:, n * 512:(n + 1) * 512], start=(f == 0), stop=False))
                bar()
            for ti in range(4):
                for n2 in range(2):
                    pp, oo = wcon_psum(ti, n2)
                    t_(lambda e, p=pp, o=oo, n=n2: e.matmul(
                        out=p[:, o:o + 512], lhsT=ones512[0:1, 0:128],
                        rhs=brows[0:1, 2048 + n * 512:2048 + (n + 1) * 512], start=False, stop=True))
            bar()
            for ti in range(4):
                xa = xb[ti % 2]
                dma("sync", "A", xa[:], xscr[(tt0 + ti) * 128:(tt0 + ti + 1) * 128, :])
                bar()
                for n2 in range(2):
                    pp, oo = wcon_psum(ti, n2)
                    v_(lambda e, a=xa, p=pp, o=oo, n=n2: e.tensor_add(
                        out=a[:, n * 512:(n + 1) * 512], in0=a[:, n * 512:(n + 1) * 512],
                        in1=p[:, o:o + 512]))
                    bar()
                dma("gpsimd", "B", xscr[(tt0 + ti) * 128:(tt0 + ti + 1) * 128, :], xa[:])
            bar()

        dma("sync", "A", tap1, xscr)
        bar()
        def qtv(cc):
            return GB[:, cc * 2048:(cc + 1) * 2048]

        def ktv(cc):
            return GC[:, cc * 2048:(cc + 1) * 2048]

        def otv(cc):
            return GA[:, cc * 2048:(cc + 1) * 2048]

        def vtm_ap(ti, off, dims):
            if ti < 15:
                return AP(GD, ti * 1040 + off, dims)
            return AP(vtmx, off, dims)

        for ti in range(NT):
            layer_norm_tile(xscr, ti, xnt[:])
            tile_to_fm(xnt, lambda c, t=ti: GA[:, c * 2048 + t * 128: c * 2048 + (t + 1) * 128])
        for pi2, (wname, bc) in enumerate([("l1wq", 0), ("l1wk", 8)]):
            for oc in range(NC):
                wb = WB1 if oc % 2 == 0 else WB2
                load_cast(wb[:], AP(D[wname].tensor, oc * 128, [[1024, 128], [131072, 8], [1, 128]]))
                for tb in range(2):
                    ps = PA if tb == 0 else PB
                    for kc in range(NC):
                        for nh2 in range(2):
                            t_(lambda e, k=kc, n=nh2, w=wb, p=ps, tbb=tb: e.matmul(
                                out=p[:, n * 512:(n + 1) * 512],
                                lhsT=w[:, k * 128:(k + 1) * 128],
                                rhs=GA[:, k * 2048 + tbb * 1024 + n * 512: k * 2048 + tbb * 1024 + (n + 1) * 512],
                                start=(k == 0), stop=(k == NC - 1)))
                bar()
                dstf = qtv if pi2 == 0 else ktv
                s_(lambda e, o=oc, b=bc, df=dstf: e.activation(
                    out=df(o)[:, 0:1024], in_=PA[:, 0:1024], func=AF.Identity,
                    bias=l1bcol[:, b + o:b + o + 1], scale=1.0))
                s_(lambda e, o=oc, b=bc, df=dstf: e.activation(
                    out=df(o)[:, 1024:2048], in_=PB[:, 0:1024], func=AF.Identity,
                    bias=l1bcol[:, b + o:b + o + 1], scale=1.0))
                bar()
        for ti in range(NT):
            ps = PA if ti % 2 == 0 else PB
            for kc in range(NC):
                wb = WB1 if kc % 2 == 0 else WB2
                load_cast(wb[:], D["l1wv"][kc * 128:(kc + 1) * 128, :])
                for nh2 in range(2):
                    t_(lambda e, k=kc, n=nh2, w=wb, p=ps, t=ti: e.matmul(
                        out=p[:, n * 512:(n + 1) * 512],
                        lhsT=GA[:, k * 2048 + t * 128: k * 2048 + (t + 1) * 128],
                        rhs=w[:, n * 512:(n + 1) * 512], start=(k == 0), stop=False))
                bar()
            for nh2 in range(2):
                t_(lambda e, n=nh2, p=ps: e.matmul(
                    out=p[:, n * 512:(n + 1) * 512], lhsT=ones512[0:1, 0:128],
                    rhs=brows[0:1, n * 512:(n + 1) * 512], start=False, stop=True))
            bar()
            s_(lambda e, t=ti, p=ps: e.copy(
                out=vtm_ap(t, 0, [[16384 if t < 15 else 1040, 128], [65, 16], [1, 64]]),
                in_=p[:, 0:1024]))
            bar()
            g_(lambda e, t=ti: e.memset(
                vtm_ap(t, 64, [[16384 if t < 15 else 1040, 128], [65, 16], [1, 1]]), 1.0))
            bar()
        for cc in range(NC):
            v_(lambda e, c=cc: e.reduce_max(out=mm8[:, c:c + 1], in_=qtv(c), axis=AX.X,
                                            apply_absolute_value=True))
            v_(lambda e, c=cc: e.reduce_max(out=mm8[:, 8 + c:8 + c + 1], in_=ktv(c), axis=AX.X,
                                            apply_absolute_value=True))
        bar()
        t_(lambda e: e.matmul(out=PA[0:16, 0:128], lhsT=mm8[:], rhs=idf[:], start=True, stop=True))
        bar()
        s_(lambda e: e.copy(out=m82[:, :], in_=PA[0:16, 0:128]))
        bar()
        v_(lambda e: e.reduce_max(out=stat2[:, 0:2], in_=m82[:].rearrange("p (h k) -> p h k", k=64),
                                  axis=AX.X))
        bar()
        dma("gpsimd", "B", AP(mscr.tensor, 0, [[2, 16], [1, 2]]), stat2[:, 0:2])
        bar()
        dma("sync", "A", mrow[0:1, 0:32], AP(mscr.tensor, 0, [[32, 1], [1, 32]]))
        bar()
        v_(lambda e: e.tensor_mul(out=mrow[0:1, 0:16], in0=mrow[0:1, 0:16], in1=mrow[0:1, 16:32]))
        bar()
        v_(lambda e: e.tensor_scalar(out=mrow[0:1, 0:16], in0=mrow[0:1, 0:16], scalar1=-8.0,
                                     scalar2=-60.0, op0=OP.mult, op1=OP.max))
        bar()
        t_(lambda e: e.matmul(out=PA[:, 0:16], lhsT=onesf[:], rhs=mrow[0:1, 0:16], start=True, stop=True))
        bar()
        s_(lambda e: e.copy(out=negM[:], in_=PA[:, 0:16]))
        bar()
        dma("sync", "A", dbg[0:128, 2000:2016], negM[:])
        dma("sync", "A", dbg[0:1, 1960:1992], mrow[0:1, 0:32])
        dma("sync", "A", dbg[0:16, 1900:1904], stat2[:])
        bar()
        for h in range(NHEAD):
            hc, hp = h // 2, (h % 2) * 64
            for qb in range(4):
                njs = 4 * qb + 4
                for jg in range(0, njs, 2):
                    for sub in range(2):
                        j = jg + sub
                        if j >= njs:
                            continue
                        ps = PC if sub == 0 else PD
                        t_(lambda e, jj=j, p=ps, c=hc, hh=hp, q=qb: e.matmul(
                            out=p[:, 0:512],
                            lhsT=ktv(c)[hh:hh + 64, jj * 128:(jj + 1) * 128],
                            rhs=qtv(c)[hh:hh + 64, q * 512:(q + 1) * 512],
                            start=True, stop=(jj < 4 * q)))
                        if j >= 4 * qb:
                            t_(lambda e, p=ps, dd=j - 4 * qb: e.matmul(
                                out=p[:, 0:512], lhsT=idbf[:],
                                rhs=maskd[:, dd * 512:(dd + 1) * 512], start=False, stop=True))
                    bar()
                    for sub in range(2):
                        j = jg + sub
                        if j >= njs:
                            continue
                        ps = PC if sub == 0 else PD
                        s_(lambda e, jj=j, p=ps, hh=h: e.activation(
                            out=ptile[:, (jj % 4) * 512:((jj % 4) + 1) * 512], in_=p[:, 0:512],
                            func=AF.Exp, bias=negM[:, hh:hh + 1], scale=0.125))
                    bar()
                    for sub in range(2):
                        j = jg + sub
                        if j >= njs:
                            continue
                        t_(lambda e, jj=j, hh=h, nn=njs: e.matmul(
                            out=PE[0:65, 0:512],
                            lhsT=vtm_ap(jj, hh * 65, [[16384 if jj < 15 else 1040, 128], [1, 65]]),
                            rhs=ptile[:, (jj % 4) * 512:((jj % 4) + 1) * 512],
                            start=(jj == 0), stop=(jj == nn - 1)))
                    bar()
                s_(lambda e, c=hc, hh=hp, q=qb: e.copy(
                    out=otv(c)[hh:hh + 64, q * 512:(q + 1) * 512], in_=PE[0:64, 0:512]))
                v_(lambda e: e.tensor_copy(out=AAF1[64:65, 0:512], in_=PE[64:65, 0:512]))
                bar()
                dma("sync", "A", AP(denscr.tensor, h * 2048 + qb * 512, [[512, 1], [1, 512]]),
                    AAF1[64:65, 0:512])
                bar()
        for cc in range(NC):
            dma("sync", "A", den2[:], AP(denscr.tensor, 2 * cc * 2048, [[2048, 2], [1, 2048]]))
            bar()
            if cc == 0:
                dma("sync", "A", dbg[0:2, 0:1800], den2[0:2, 0:1800])
                bar()
            v_(lambda e: e.reciprocal(out=den2[:], in_=den2[:]))
            bar()
            for ti in range(NT):
                t_(lambda e, t=ti: e.matmul(out=PC[:, 0:128], lhsT=ind2[:],
                                            rhs=den2[:, t * 128:(t + 1) * 128], start=True, stop=True))
                bar()
                v_(lambda e, c=cc, t=ti: e.tensor_mul(
                    out=otv(c)[:, t * 128:(t + 1) * 128],
                    in0=otv(c)[:, t * 128:(t + 1) * 128], in1=PC[:, 0:128]))
                bar()
        for kc in range(NC):
            load_cast(GB[:, kc * 1024:(kc + 1) * 1024], D["l1wo"][kc * 128:(kc + 1) * 128, :])
        for ti in range(NT):
            ps = PA if ti % 2 == 0 else PB
            xa = xb[ti % 2]
            dma("gpsimd", "B", xa[:], xscr[ti * 128:(ti + 1) * 128, :])
            for kc in range(NC):
                for nh2 in range(2):
                    t_(lambda e, k=kc, n=nh2, p=ps, t=ti: e.matmul(
                        out=p[:, n * 512:(n + 1) * 512],
                        lhsT=otv(k)[:, t * 128:(t + 1) * 128],
                        rhs=GB[:, k * 1024 + n * 512:k * 1024 + (n + 1) * 512],
                        start=(k == 0), stop=False))
            bar()
            for nh2 in range(2):
                t_(lambda e, n=nh2, p=ps: e.matmul(
                    out=p[:, n * 512:(n + 1) * 512], lhsT=ones512[0:1, 0:128],
                    rhs=brows[0:1, 1024 + n * 512:1024 + (n + 1) * 512], start=False, stop=True))
            bar()
            v_(lambda e, p=ps, a=xa: e.tensor_add(out=a[:], in0=a[:], in1=p[:, 0:1024]))
            bar()
            dma("gpsimd", "B", xscr[ti * 128:(ti + 1) * 128, :], xa[:])
        bar()

        dma("sync", "A", tap2, xscr)
        bar()
        def quantize(dst_bf):
            v_(lambda e: e.reduce_max(out=q_sc[:], in_=WSb[:, 0:1024].rearrange("p (a b) -> p a b", b=32),
                                      axis=AX.X, apply_absolute_value=True))
            bar()
            v_(lambda e: e.tensor_scalar(out=q_sc[:], in0=q_sc[:], scalar1=1.0 / 7.5,
                                         scalar2=1.0e-30, op0=OP.mult, op1=OP.max))
            bar()
            v_(lambda e: e.reciprocal(out=q_isc[:], in_=q_sc[:]))
            bar()
            v_(lambda e: e.tensor_mul(out=WSa[:, 0:1024].rearrange("p (a b) -> p a b", b=32),
                                      in0=WSb[:, 0:1024].rearrange("p (a b) -> p a b", b=32),
                                      in1=AP(q_isc, 0, [[32, 128], [1, 32], [0, 32]])))
            bar()
            s_(lambda e: e.activation(out=WSa[:, 1024:2048], in_=WSa[:, 0:1024], func=AF.Abs,
                                      bias=0.0, scale=1.0))
            bar()
            v_(lambda e: e.tensor_single_scalar(out=WSa[:, 2048:3072], in_=WSa[:, 1024:2048],
                                                scalar=2.0, op=OP.is_ge))
            v_(lambda e: e.tensor_single_scalar(out=WSa[:, 3072:4096], in_=WSa[:, 1024:2048],
                                                scalar=4.0, op=OP.is_ge))
            bar()
            v_(lambda e: e.tensor_scalar(out=WSb[:, 1024:2048], in0=WSa[:, 3072:4096], scalar1=-2.0,
                                         scalar2=8.0, op0=OP.mult, op1=OP.add))
            bar()
            v_(lambda e: e.scalar_tensor_tensor(out=WSb[:, 1024:2048], in0=WSa[:, 2048:3072],
                                                scalar=-4.0, in1=WSb[:, 1024:2048], op0=OP.mult, op1=OP.add))
            bar()
            v_(lambda e: e.tensor_mul(out=WSa[:, 0:1024], in0=WSa[:, 0:1024], in1=WSb[:, 1024:2048]))
            bar()
            v_(lambda e: e.tensor_single_scalar(out=WSa[:, 1024:2048], in_=WSa[:, 0:1024],
                                                scalar=MAGIC, op=OP.add))
            bar()
            v_(lambda e: e.tensor_single_scalar(out=WSa[:, 1024:2048], in_=WSa[:, 1024:2048],
                                                scalar=MAGIC, op=OP.subtract))
            bar()
            v_(lambda e: e.tensor_scalar(out=WSa[:, 2048:3072], in0=WSa[:, 2048:3072], scalar1=0.125,
                                         scalar2=0.125, op0=OP.mult, op1=OP.add))
            bar()
            v_(lambda e: e.scalar_tensor_tensor(out=WSa[:, 2048:3072], in0=WSa[:, 3072:4096],
                                                scalar=0.25, in1=WSa[:, 2048:3072], op0=OP.mult, op1=OP.add))
            bar()
            v_(lambda e: e.tensor_mul(out=WSa[:, 2048:3072].rearrange("p (a b) -> p a b", b=32),
                                      in0=WSa[:, 2048:3072].rearrange("p (a b) -> p a b", b=32),
                                      in1=AP(q_sc, 0, [[32, 128], [1, 32], [0, 32]])))
            bar()
            v_(lambda e, d=dst_bf: e.tensor_mul(out=d, in0=WSa[:, 1024:2048], in1=WSa[:, 2048:3072]))
            bar()

        for ch in range(NC):
            dma("sync", "A", WSb[:, 0:1024], D["l1fw"][ch * 128:(ch + 1) * 128, :])
            bar()
            quantize(GB[:, 8192 + ch * 1024:8192 + (ch + 1) * 1024])
        for oc in range(NC):
            for grp in range(2):
                for k in range(4):
                    t_(lambda e, d=grp * 4 + k, o=oc, kk=k: e.matmul(
                        out=PA[:, kk * 128:(kk + 1) * 128],
                        lhsT=GB[:, 8192 + d * 1024 + o * 128:8192 + d * 1024 + (o + 1) * 128],
                        rhs=idbf[:], start=True, stop=True))
                bar()
                for k in range(4):
                    s_(lambda e, d=grp * 4 + k, o=oc, kk=k: e.copy(
                        out=GB[:, o * 1024 + d * 128:o * 1024 + (d + 1) * 128],
                        in_=PA[:, kk * 128:(kk + 1) * 128]))
                bar()
        for ti in range(NT):
            layer_norm_tile(xscr, ti, xnt[:], out_f32_ap=WSb[:, 0:1024])
            quantize(GA[:, ti * 1024:(ti + 1) * 1024])
        for ti in range(NT):
            for grp in range(2):
                for k in range(4):
                    t_(lambda e, c=grp * 4 + k, t=ti, kk=k: e.matmul(
                        out=PA[:, kk * 128:(kk + 1) * 128],
                        lhsT=GA[:, t * 1024 + c * 128:t * 1024 + (c + 1) * 128],
                        rhs=idbf[:], start=True, stop=True))
                bar()
                for k in range(4):
                    s_(lambda e, c=grp * 4 + k, t=ti, kk=k: e.copy(
                        out=GC[:, c * 2048 + t * 128:c * 2048 + (t + 1) * 128],
                        in_=PA[:, kk * 128:(kk + 1) * 128]))
                bar()
        for ti in range(NT):
            ps = PA if ti % 2 == 0 else PB
            xa = xb[ti % 2]
            dma("gpsimd", "B", xa[:], xscr[ti * 128:(ti + 1) * 128, :])
            for kc in range(NC):
                for nh2 in range(2):
                    t_(lambda e, k=kc, n=nh2, p=ps, t=ti: e.matmul(
                        out=p[:, n * 512:(n + 1) * 512],
                        lhsT=GC[:, k * 2048 + t * 128:k * 2048 + (t + 1) * 128],
                        rhs=GB[:, k * 1024 + n * 512:k * 1024 + (n + 1) * 512],
                        start=(k == 0), stop=False))
            bar()
            for nh2 in range(2):
                t_(lambda e, n=nh2, p=ps: e.matmul(
                    out=p[:, n * 512:(n + 1) * 512], lhsT=ones512[0:1, 0:128],
                    rhs=l1fbrow[0:1, n * 512:(n + 1) * 512], start=False, stop=True))
            bar()
            v_(lambda e, p=ps, a=xa: e.tensor_add(out=a[:], in0=a[:], in1=p[:, 0:1024]))
            bar()
            dma("gpsimd", "B", xout[ti * 128:(ti + 1) * 128, :], xa[:])
        bar()
        prog.on("sync", lambda e: e.wait_ge(dA, 16 * prog.dma_issued["A"]))
        prog.on("sync", lambda e: e.wait_ge(dB, 16 * prog.dma_issued["B"]))

        emit(prog, block, bsem, dsems)
    return nc


def host_prepare(inp, core):
    import ml_dtypes
    d = {}
    d["x_in"] = np.ascontiguousarray(inp["L0_attn_x"][core])
    d["wkT"] = np.ascontiguousarray(inp["L0_attn_Wk"].T)
    d["wvT"] = np.ascontiguousarray(inp["L0_attn_Wv"].T)
    d["wrT"] = np.ascontiguousarray(inp["L0_attn_Wr"].T)
    d["woT"] = np.ascontiguousarray(inp["L0_attn_Wo"].T)
    colv = np.zeros((128, 40), np.float32)
    colv[:, 0:8] = inp["L0_attn_time_decay"].reshape(8, 128).T
    colv[:, 8:16] = inp["L0_attn_time_first"].reshape(8, 128).T
    colv[:, 16:24] = inp["L0_attn_mix_k"].reshape(8, 128).T
    colv[:, 24:32] = inp["L0_attn_mix_v"].reshape(8, 128).T
    colv[:, 32:40] = inp["L0_attn_mix_r"].reshape(8, 128).T
    d["colv"] = colv
    d["wexp"] = np.ascontiguousarray(inp["L0_ffn_W_exp"])
    fp2f = np.empty(HID, np.int64)
    for fp in range(HID):
        fp2f[fp] = (fp % 32) * 128 + fp // 32
    d["wconP"] = np.ascontiguousarray(inp["L0_ffn_W_con"][fp2f])
    d["becol"] = np.ascontiguousarray(inp["L0_ffn_b_exp"].reshape(32, 128).T)
    d["ubPcol"] = np.ascontiguousarray(inp["L0_ffn_up_bias"][fp2f].reshape(32, 128).T)

    bf_init = np.zeros((4, 128, 1024), np.float32)
    bf_wex = np.zeros((4, 4, 4, 128, 512), np.float32)
    for bi, wkey in enumerate(["L0_ffn_up_weights", "L0_ffn_down_weights"]):
        W = inp[wkey]

        def wval(st, f, h1, h2):
            s = 2 ** st
            k = (f // (2 * s)) * s + f % s
            return W[st, k, h1, h2]
        ini = np.zeros((128, 32, 32), np.float32)
        ar = np.arange(32)
        for p in range(128):
            f = p * 32 + ar
            h = ar % 2
            for a in range(32):
                ini[p, a, a] = wval(0, f[a], h[a], h[a])
                ini[p, a, a ^ 1] = wval(0, f[a], h[a], 1 - h[a])
        bf_init[bi * 2] = ini.reshape(128, 1024)
        for si in range(4):
            st = si + 1
            s = 2 ** st
            wex = np.zeros((4, 128, 16, 32), np.float32)
            for p in range(128):
                for am in range(16):
                    b2, pos = am // s, am % s
                    for h in range(2):
                        a = b2 * 2 * s + h * s + pos
                        f = p * 32 + a
                        wd = wval(st, f, h, h)
                        wo = wval(st, f, h, 1 - h)
                        if h == 0:
                            wex[0, p, am, :] = wd
                            wex[1, p, am, :] = wo
                        else:
                            wex[3, p, am, :] = wd
                            wex[2, p, am, :] = wo
            bf_wex[bi * 2, si] = wex.reshape(4, 128, 512)
        def wvalB(st, cpos, l, h1, h2):
            f = cpos * 128 + l
            s = 2 ** st
            k = (f // (2 * s)) * s + f % s
            return W[st, k, h1, h2]
        ini = np.zeros((128, 32, 32), np.float32)
        sj = 16
        for p in range(128):
            for a in range(32):
                h = (a // sj) % 2
                ini[p, a, a] = wvalB(11, a, p, h, h)
                ini[p, a, a ^ sj] = wvalB(11, a ^ sj, p, h ^ 1, h) if False else wvalB(11, a, p, 1 - h, h)
        bf_init[bi * 2 + 1] = ini.reshape(128, 1024)
        for si in range(4):
            st = 10 - si
            sj = 2 ** (st - 7)
            wex = np.zeros((4, 128, 16, 32), np.float32)
            for p in range(128):
                for am in range(16):
                    b2, pos = am // sj, am % sj
                    for h in range(2):
                        a = b2 * 2 * sj + h * sj + pos
                        wd = wvalB(st, a, p, h, h)
                        wo = wvalB(st, a, p, 1 - h, h)
                        if h == 0:
                            wex[0, p, am, :] = wd
                            wex[1, p, am, :] = wo
                        else:
                            wex[3, p, am, :] = wd
                            wex[2, p, am, :] = wo
            bf_wex[bi * 2 + 1, si] = wex.reshape(4, 128, 512)
    d["bf_init"] = bf_init
    d["bf_wex"] = bf_wex
    bf56_d = np.zeros((2, 2, 128, 32), np.float32)
    bf56_o = np.zeros((2, 2, 128, 32), np.float32)
    for bi, wkey in enumerate(["L0_ffn_up_weights", "L0_ffn_down_weights"]):
        W = inp[wkey]
        for sti, st in enumerate((5, 6)):
            s = 2 ** st
            for p in range(128):
                for c in range(32):
                    f = c * 128 + p
                    k = (f // (2 * s)) * s + f % s
                    h = (f // s) % 2
                    bf56_d[bi, sti, p, c] = W[st, k, h, h]
                    bf56_o[bi, sti, p, c] = W[st, k, h, 1 - h]
    d["bf56_d"] = bf56_d
    d["bf56_o"] = bf56_o
    d["l1wq"] = np.ascontiguousarray(inp["L1_attn_Wq"])
    d["l1wk"] = np.ascontiguousarray(inp["L1_attn_Wk"])
    d["l1wv"] = np.ascontiguousarray(inp["L1_attn_Wv"])
    d["l1wo"] = np.ascontiguousarray(inp["L1_attn_Wo"])
    l1b = np.zeros((128, 16), np.float32)
    l1b[:, 0:8] = inp["L1_attn_bq"].reshape(8, 128).T
    l1b[:, 8:16] = inp["L1_attn_bk"].reshape(8, 128).T
    d["l1bcol"] = l1b
    brows = np.zeros((1, 3072), np.float32)
    brows[0, 0:1024] = inp["L1_attn_bv"]
    brows[0, 1024:2048] = inp["L1_attn_bo"]
    brows[0, 2048:3072] = inp["L0_ffn_b_con"]
    d["brows"] = brows
    d["dbrow"] = inp["L0_ffn_down_bias"].reshape(1, C)
    d["l1fbrow"] = inp["L1_ffn_bias"].reshape(1, C)
    d["l1fw"] = np.ascontiguousarray(inp["L1_ffn_weight"])
    d["c_idbf"] = np.eye(128, dtype=ml_dtypes.bfloat16)
    d["c_idf"] = np.eye(128, dtype=np.float32)
    cm = np.zeros((4, 128, 512), np.float32)
    for dd in range(4):
        p = np.arange(128)[:, None]
        ql = np.arange(512)[None, :]
        cm[dd] = np.where(p + dd * 128 > ql, NEG, 0.0)
    d["c_mask"] = cm.astype(ml_dtypes.bfloat16)
    pi = np.zeros((2, 128, 128), np.float32)
    for sti, s in enumerate((32, 64)):
        for i in range(128):
            pi[sti, i, i ^ s] = 1.0
    d["c_pi"] = pi
    ind = np.zeros((2, 128), np.float32)
    ind[0, 0:64] = 1.0
    ind[1, 64:128] = 1.0
    d["c_ind2"] = ind
    d["c_ones512"] = np.ones((1, 512), ml_dtypes.bfloat16)
    d["c_onesf"] = np.ones((1, 128), np.float32)
    return d


def kernel(**inputs):
    nc = build()
    in_maps = [host_prepare(inputs, core) for core in range(8)]
    res = run_bass_kernel_spmd(nc, in_maps, core_ids=list(range(8)))
    out = np.stack([r["x_out"] for r in res.results], axis=0)
    return out.astype(np.float32)
